# version 27
# baseline (speedup 1.0000x reference)
import numpy as np
import concourse.bass as bass
import concourse.mybir as mybir
from concourse.bass_utils import run_bass_kernel_spmd

F32 = mybir.dt.float32
BF16 = mybir.dt.bfloat16
AL = mybir.AluOpType
AF = mybir.ActivationFunctionType
AX = mybir.AxisListType

ENGINES = ("sync", "gpsimd", "scalar", "vector", "tensor")

S = 2048
H = 2048
FC = 40
INV2048 = 1.0 / 2048.0


class Builder:
    def __init__(self, nc):
        self.nc = nc
        self.ops = {e: [] for e in ENGINES}
        self.plan = {}
        self.sem_names = []
        self.last_tok = {e: None for e in ENGINES}

    def sem(self, name):
        if name not in self.plan:
            self.plan[name] = 0
            self.sem_names.append(name)

    ENGSEM = {"vector": "sV", "scalar": "sA", "gpsimd": "sG"}

    def op(self, eng, fn, waits=(), inc=None, n=1):
        self_wait = None
        if eng in self.ENGSEM:
            sname = self.ENGSEM[eng]
            self.sem(sname)
            prev = self.plan[sname]
            if prev > 0:
                self_wait = (sname, prev, "_self")
            if inc is None:
                inc = sname
        tok = None
        if inc is not None:
            self.sem(inc)
            self.plan[inc] += n
            tok = (inc, self.plan[inc], eng)
            self.last_tok[eng] = tok
        ws = []
        for w in list(waits) + [self_wait]:
            if w is None:
                continue
            sname, v, peng = w
            ws.append((sname, v))
        self.ops[eng].append((fn, ws, (inc, n) if inc is not None else None))
        return tok

    def dma(self, out, in_, waits=(), stream="dL", eng="sync"):
        tok = self.op(eng, lambda e: e.dma_start(out=out, in_=in_),
                      waits=waits, inc="D_" + stream, n=16)
        if not hasattr(self, "stream_last"):
            self.stream_last = {}
        self.stream_last[stream] = tok
        return tok

    def finalize(self):
        nc = self.nc
        import contextlib
        stack = contextlib.ExitStack()
        handles = {}
        for name in self.sem_names:
            handles[name] = stack.enter_context(nc.semaphore(name))
        with stack:
            with nc.Block() as block:
                def make(eng_name):
                    def run(eng):
                        satisfied = {}
                        for fn, ws, inc in self.ops[eng_name]:
                            for sname, v in ws:
                                if satisfied.get(sname, -1) >= v:
                                    continue
                                eng.wait_ge(handles[sname], v)
                                satisfied[sname] = v
                            ins = fn(eng)
                            if inc is not None:
                                ins.then_inc(handles[inc[0]], inc[1])
                    return run
                for eng_name in ENGINES:
                    getattr(block, eng_name)(make(eng_name))


def build_core():
    nc = bass.Bass("TRN2", target_bir_lowering=False)

    x_in = nc.dram_tensor("x", [S, H], F32, kind="ExternalInput").ap()
    wqk = nc.dram_tensor("wqk", [48 * 128, 2048], F32, kind="ExternalInput").ap()
    bqkv_in = nc.dram_tensor("bqkv", [128, 48], F32, kind="ExternalInput").ap()
    bout_in = nc.dram_tensor("bout", [1, 2048], F32, kind="ExternalInput").ap()
    wot = nc.dram_tensor("wot", [16 * 128, 2048], F32, kind="ExternalInput").ap()
    wfct = nc.dram_tensor("wfct", [16 * 128, 5120], F32, kind="ExternalInput").ap()
    wprt = nc.dram_tensor("wprt", [40 * 128, 2048], F32, kind="ExternalInput").ap()
    out_hbm = nc.dram_tensor("out", [S, H], F32, kind="ExternalOutput").ap()

    qf = nc.dram_tensor("qf", [16 * 4 * 128, 1024], BF16).ap()
    kf = nc.dram_tensor("kf", [16 * 2 * 128, 2048], BF16).ap()
    vt = nc.dram_tensor("vt", [16 * 2 * 128, 2048], BF16).ap()
    wos = nc.dram_tensor("wos", [16 * 2 * 128, 2048], BF16).ap()
    qfc = nc.dram_tensor("qfc", [16 * 128, 5120], BF16).ap()
    qpr = nc.dram_tensor("qpr", [40 * 128, 2048], BF16).ap()

    AR_B = 200704
    my_base = (nc.sbuf_top - AR_B) & ~31
    _bufs = {}

    def _at(off, n, dt, tag):
        key = (off, n, tag)
        if key not in _bufs:
            _bufs[key] = nc.alloc_sbuf_tensor_at(
                "%s_%d_%d" % (tag, off, n), [128, n], dt,
                offset=my_base + off).ap()
        return _bufs[key]

    def bb(off, n):
        assert off % 2 == 0 and off + 2 * n <= AR_B, (off, n)
        return _at(off, n, BF16, "ab")

    def bf(off, n):
        assert off % 4 == 0 and off + 4 * n <= AR_B, (off, n)
        return _at(off, n, F32, "af")

    idn = nc.alloc_sbuf_tensor("idn", [128, 128], F32).ap()
    ones_row = nc.alloc_sbuf_tensor("ones_row", [1, 128], F32).ap()
    ones_bf = nc.alloc_sbuf_tensor("ones_bf", [1, 128], BF16).ap()
    bout_hi = nc.alloc_sbuf_tensor("bout_hi", [1, 2048], BF16).ap()
    bout_lo = nc.alloc_sbuf_tensor("bout_lo", [1, 2048], BF16).ap()
    bqkv_sb = nc.alloc_sbuf_tensor("bqkv_sb", [128, 48], F32).ap()
    st = nc.alloc_sbuf_tensor("st", [128, 128], F32).ap()

    pb = [nc.alloc_psum_tensor("pb%d" % i, [128, 512], F32).ap() for i in range(8)]

    B = Builder(nc)
    for s_ in ("dL", "dS", "dO", "sP", "sA", "sV", "sG"):
        B.sem(s_)

    ONESC, SSCB, EPSB = 0, 1, 2
    THRF, NTHRF, THRP, NTHRP, SFCB, SPRB = 3, 4, 5, 6, 7, 8
    NMAX, EBIAS, ESUM, RQ = 9, 10, 11, 12
    NSUM, NM, SSQ, T1, T2M, VAR, SD, RSTD, NMR = 13, 14, 15, 16, 17, 18, 19, 20, 21
    TMP0, ACC0 = 22, 23
    C2048, CN, C7, C10, C1E5, CZERO = 40, 41, 42, 43, 44, 45
    SQ2048, SSC, C07, EPS = 46, 47, 48, 49
    TOTF, MEANF, AMF, THRFs, NTHRFs = 50, 51, 52, 53, 54
    TOTP, MEANP, AMP, THRPs, NTHRPs = 55, 56, 57, 58, 59
    TMP1, RN = 60, 61

    def cell(c):
        return st[0:1, c:c + 1]

    def vcol(c):
        return st[:, c:c + 1]

    t_mi = B.op("gpsimd", lambda e: e.memset(idn, 0.0), inc="sG")
    B.op("gpsimd", lambda e: e.affine_select(
        out=idn, in_=idn, compare_op=AL.not_equal, fill=1.0, base=0,
        pattern=[[-1, 128]], channel_multiplier=1), waits=[t_mi], inc="sG")
    B.op("gpsimd", lambda e: e.memset(ones_row, 1.0), inc="sG")
    B.op("gpsimd", lambda e: e.memset(vcol(ONESC), 1.0), inc="sG")
    for c, v in ((C2048, 2048.0), (CN, 10485760.0), (C7, 7.0), (C10, 10.0),
                 (C1E5, 100000.0), (CZERO, 0.0)):
        B.op("gpsimd", lambda e, c=c, v=v: e.memset(cell(c), v), inc="sG")
    t_g0 = ("sG", B.plan["sG"], "gpsimd")

    t_b1 = B.dma(bqkv_sb, bqkv_in[:, :], stream="b0")
    bout_stage = bf(131072, 2048)[0:1, :]
    t_b2 = B.dma(bout_stage, bout_in[:, :], stream="b1")
    B.op("gpsimd", lambda e: e.memset(ones_bf, 1.0), inc="sG")
    t_bh = B.op("scalar", lambda e: e.activation(bout_hi, bout_stage, AF.Copy),
                waits=[t_b2], inc="sA")
    t_bl = B.op("vector", lambda e: e.tensor_tensor(bout_lo, bout_stage, bout_hi,
                                                    op=AL.subtract),
                waits=[t_b2, t_bh], inc="sV")

    t_sq0 = B.op("scalar", lambda e: e.activation(cell(SQ2048), cell(C2048), AF.Sqrt),
                 waits=[t_g0], inc="sA")
    t_ssc = B.op("vector", lambda e: e.reciprocal(cell(SSC), cell(SQ2048)),
                 waits=[t_sq0, t_g0], inc="sV")
    t_r10 = B.op("vector", lambda e: e.reciprocal(cell(TMP1), cell(C10)), waits=[t_g0],
                 inc="sV")
    t_c07 = B.op("vector", lambda e: e.tensor_tensor(cell(C07), cell(C7), cell(TMP1),
                                                     op=AL.mult), waits=[t_r10], inc="sV")
    t_rn = B.op("vector", lambda e: e.reciprocal(cell(RN), cell(CN)), inc="sV")
    t_eps = B.op("vector", lambda e: e.reciprocal(cell(EPS), cell(C1E5)), inc="sV")

    bank_last = {i: None for i in range(8)}

    def bcast(src_cell, dst_col, bank, waits):
        t_mm = B.op("tensor", lambda e, s=src_cell, b=bank: e.matmul(
            pb[b][:, 0:1], ones_row, s, start=True, stop=True),
            waits=list(waits) + [bank_last[bank], t_g0], inc="sP")
        t_cp = B.op("vector", lambda e, d=dst_col, b=bank: e.tensor_copy(
            vcol(d), pb[b][:, 0:1]), waits=[t_mm], inc="sV")
        bank_last[bank] = t_cp
        return t_cp

    t_sscb = bcast(cell(SSC), SSCB, 0, [t_ssc])
    t_epsb = bcast(cell(EPS), EPSB, 1, [t_eps])

    TZS = 167936
    tz_wf = [bf(TZS + k * 4096, 1024) for k in range(4)]
    tz_ta = bf(TZS + 16384, 1024)
    tz_tb = bf(TZS + 20480, 1024)
    tz_q = [bb(TZS + 24576, 1024), bb(TZS + 26624, 1024)]

    tern_done = {}
    tern_scale = {}
    tern_final = []

    def bcast_b(src_cell, dst_col, bank_free, bk, waits):
        t_mm = B.op("tensor", lambda e, s=src_cell, b=bk: e.matmul(
            pb[b][:, 0:1], ones_row, s, start=True, stop=True),
            waits=list(waits) + [bank_free[bk]], inc="sP")
        t_cp = B.op("vector", lambda e, d=dst_col, b=bk: e.tensor_copy(
            vcol(d), pb[b][:, 0:1]), waits=[t_mm], inc="sV")
        bank_free[bk] = t_cp
        return t_cp

    def tern_units(bank_free):
        P1F = [(wfct, i, t0) for i in range(16) for t0 in range(0, 5120, 1024)]
        P1P = [(wprt, i, t0) for i in range(40) for t0 in range(0, 2048, 1024)]
        sched = P1F + P1P + P1F + P1P
        n1 = len(P1F) + len(P1P)
        consumer_tok = [None] * 4
        load_tok = [None] * len(sched)
        n_loaded = [0]

        def issue_load(k):
            sl = k % 4
            wsrc, i, t0 = sched[k]
            load_tok[k] = B.dma(tz_wf[sl], wsrc[i * 128:(i + 1) * 128, t0:t0 + 1024],
                                waits=[consumer_tok[sl]], stream="tzs%d" % sl,
                                eng="gpsimd")
            n_loaded[0] += 1

        for k in range(min(4, len(sched))):
            issue_load(k)

        t_mz = B.op("gpsimd", lambda e: e.memset(vcol(ACC0), 0.0),
                    waits=[t_g0], inc="sG")
        stats = {}
        t_a = None
        t_q = None
        q_free = [None, None]
        for k in range(len(sched)):
            sl = k % 4
            buf = tz_wf[sl]
            if k < n1:
                t_ab = B.op("scalar", lambda e, b=buf: e.activation(
                    tz_ta, b, AF.Abs, accum_out=vcol(TMP0)),
                    waits=[load_tok[k], t_mz, t_a], inc="sA")
                t_a = B.op("vector", lambda e: e.tensor_tensor(
                    vcol(ACC0), vcol(ACC0), vcol(TMP0), op=AL.add),
                    waits=[t_ab], inc="sV")
                consumer_tok[sl] = t_ab
            else:
                kk = k - n1
                wi = 0 if kk < len(P1F) else 1
                wsrc, i, t0 = sched[k]
                dst = qfc if wi == 0 else qpr
                THRv, NTHRv = (THRF, NTHRF) if wi == 0 else (THRP, NTHRP)
                qsl_ = (k % 2)
                qb = tz_q[qsl_]
                t_s1 = B.op("scalar", lambda e, b=buf, tc=NTHRv: e.activation(
                    tz_ta, b, AF.Sign, bias=vcol(tc)),
                    waits=[load_tok[k], t_q] + (stats[wi] if (kk == 0 or kk == len(P1F)) else []),
                    inc="sA")
                t_s2 = B.op("scalar", lambda e, b=buf, tc=THRv: e.activation(
                    tz_tb, b, AF.Sign, bias=vcol(tc)), inc="sA")
                t_q = B.op("vector", lambda e, qb=qb: e.tensor_tensor(
                    qb, tz_ta, tz_tb, op=AL.add),
                    waits=[q_free[qsl_], t_s2], inc="sV")
                q_free[qsl_] = B.dma(dst[i * 128:(i + 1) * 128, t0:t0 + 1024], qb,
                                     waits=[t_q], stream="tw%d" % qsl_, eng="gpsimd")
                consumer_tok[sl] = t_s2
                if kk == len(P1F) - 1:
                    tern_done[0] = [B.stream_last.get("tw0"), B.stream_last.get("tw1")]
            if n_loaded[0] < len(sched):
                issue_load(n_loaded[0])
            if k == len(P1F) - 1 or k == n1 - 1:
                wi = 0 if k == len(P1F) - 1 else 1
                TOT, MEAN, AM, THRc, NTHRc = ((TOTF, MEANF, AMF, THRFs, NTHRFs) if wi == 0
                                              else (TOTP, MEANP, AMP, THRPs, NTHRPs))
                THRv, NTHRv, Sv = ((THRF, NTHRF, SFCB) if wi == 0 else (THRP, NTHRP, SPRB))
                t_tot = B.op("gpsimd", lambda e, TOT=TOT: e.tensor_reduce(
                    cell(TOT), vcol(ACC0), axis=AX.C, op=AL.add),
                    waits=[t_a], inc="sG")
                t_m = B.op("vector", lambda e, TOT=TOT, MEAN=MEAN: e.tensor_tensor(
                    cell(MEAN), cell(TOT), cell(RN), op=AL.mult),
                    waits=[t_tot, t_rn], inc="sV")
                B.op("vector", lambda e, MEAN=MEAN, AM=AM: e.tensor_tensor(
                    cell(AM), cell(MEAN), cell(EPS), op=AL.max), waits=[t_eps, t_m])
                B.op("vector", lambda e, AM=AM, THRc=THRc: e.tensor_tensor(
                    cell(THRc), cell(AM), cell(C07), op=AL.mult), waits=[t_c07])
                t_th = B.op("vector", lambda e, THRc=THRc, NTHRc=NTHRc: e.tensor_tensor(
                    cell(NTHRc), cell(CZERO), cell(THRc), op=AL.subtract), inc="sV")
                stats[wi] = [
                    bcast_b(cell(THRc), THRv, bank_free, 0, [t_th]),
                    bcast_b(cell(NTHRc), NTHRv, bank_free, 1, [t_th]),
                    bcast_b(cell(AM), Sv, bank_free, 2, [t_th]),
                ]
                tern_scale[wi] = B.op("vector", lambda e, Sv=Sv: e.tensor_single_scalar(
                    vcol(Sv), vcol(Sv), 0.5, op=AL.mult), waits=[stats[wi][2]], inc="sV")
                if wi == 0:
                    t_mz = B.op("gpsimd", lambda e: e.memset(vcol(ACC0), 0.0),
                                waits=[t_tot], inc="sG")
                    t_a = None
            yield
        tern_done[1] = [B.stream_last.get("tw0"), B.stream_last.get("tw1")]
        tern_final.extend([consumer_tok[0], consumer_tok[1], consumer_tok[2],
                           consumer_tok[3], q_free[0], q_free[1], t_q])

    XH, XL = 0, 65536
    XROW = [131072, 139264]
    xh = bb(XH, 32768)
    xl = bb(XL, 32768)
    bank_free = dict(bank_last)
    prev_row = [None, None]
    xrow_free = [None, None]
    for sci in range(16):
        xrow = bf(XROW[sci % 2], 2048)
        t_ld = B.dma(xrow, x_in[sci * 128:(sci + 1) * 128, :],
                     waits=[prev_row[sci % 2], t_bl if sci == 0 else None],
                     stream="xr%d" % (sci % 2))
        for hc in range(16):
            bk = (sci * 16 + hc) % 2
            t_tr = B.op("tensor", lambda e, xr=xrow, hc=hc, bk=bk: e.transpose(
                pb[bk][:, 0:128], xr[:, hc * 128:(hc + 1) * 128], idn),
                waits=[t_ld, bank_free[bk], t_g0], inc="sP")
            dst = hc * 2048 + sci * 128
            t_hi = B.op("scalar", lambda e, dst=dst, bk=bk: e.activation(
                xh[:, dst:dst + 128], pb[bk][:, 0:128], AF.Copy),
                waits=[t_tr], inc="sA")
            t_lo = B.op("vector", lambda e, dst=dst, bk=bk: e.tensor_tensor(
                xl[:, dst:dst + 128], pb[bk][:, 0:128], xh[:, dst:dst + 128],
                op=AL.subtract), waits=[t_tr, t_hi], inc="sV")
            bank_free[bk] = t_lo
            if hc == 15:
                prev_row[sci % 2] = t_tr
                xrow_free[sci % 2] = t_tr

    WQF = [131072, 139264]
    WQH = [147456, 155648]
    WQL = [151552, 159744]
    EH = [163840, 165888]
    EL = [164864, 166912]

    bank_free = {i: bank_free.get(i) for i in range(8)}
    tern_it = tern_units(bank_free)

    def pump(n):
        for _ in range(n):
            if next(tern_it, "done") == "done":
                break

    prev_pair = [None, None]
    prev_wqf = [xrow_free[0], xrow_free[1]]
    ev_free = [None, None]
    split_tok = [None, None]

    load_tok_w = [None, None]

    def qk_load(oc):
        wqf = bf(WQF[oc % 2], 2048)
        st_ = split_tok[oc % 2]
        load_tok_w[oc % 2] = B.dma(wqf, wqk[oc * 128:(oc + 1) * 128, :],
                                   waits=[prev_wqf[oc % 2],
                                          st_[1] if st_ else None],
                                   stream="wq%d" % (oc % 2))

    def qk_split(oc):
        wqf = bf(WQF[oc % 2], 2048)
        whi = bb(WQH[oc % 2], 2048)
        wlo = bb(WQL[oc % 2], 2048)
        t_hi = B.op("scalar", lambda e, w=wqf, h=whi: e.activation(h, w, AF.Copy),
                    waits=[load_tok_w[oc % 2], prev_pair[oc % 2]], inc="sA")
        t_lo = B.op("vector", lambda e, w=wqf, h=whi, l=wlo: e.tensor_tensor(
            l, w, h, op=AL.subtract),
            waits=[load_tok_w[oc % 2], t_hi, prev_pair[oc % 2]], inc="sV")
        split_tok[oc % 2] = (t_hi, t_lo)

    WOSF, WOSH, WOSL = 167936, 172032, 174080
    wos_state = {"tok": None, "wr_hi": None, "wr_lo": None}

    def wos_units():
        wosf = bf(WOSF, 1024)
        woshi = bb(WOSH, 1024)
        woslo = bb(WOSL, 1024)
        for hc in range(16):
            for oh in range(2):
                t_ld = B.dma(wosf, wot[hc * 128:(hc + 1) * 128, oh * 1024:(oh + 1) * 1024],
                             waits=[wos_state["tok"]] +
                             (tern_final if hc == 0 and oh == 0 else []),
                             stream="wsf")
                t_hi = B.op("scalar", lambda e: e.activation(woshi, wosf, AF.Copy),
                            waits=[t_ld, wos_state["wr_hi"]], inc="sA")
                t_lo = B.op("vector", lambda e: e.tensor_tensor(
                    woslo, wosf, woshi, op=AL.subtract),
                    waits=[t_ld, t_hi, wos_state["wr_lo"]], inc="sV")
                r0 = (hc * 2 + oh) * 128
                wos_state["wr_hi"] = B.dma(wos[r0:r0 + 128, 0:1024], woshi,
                                           waits=[t_hi], stream="ww0")
                wos_state["wr_lo"] = B.dma(wos[r0:r0 + 128, 1024:2048], woslo,
                                           waits=[t_lo], stream="ww1")
                wos_state["tok"] = t_lo
                yield

    wos_it = wos_units()
    VSTG = [180224, 182272]
    VEH = [184320, 185344, 186368, 187392]
    VEL = [188416, 189440, 190464, 191488]
    vstg_free = [None, None]
    veh_free = [None] * 4
    vel_free = [None] * 4
    vt4 = vt.rearrange("(kc hh p) c -> kc hh p c", hh=2, p=128)

    v_pending = []

    def v_emit(oc, ss, bk, ei, t_cp):
        hcv = oc - 32
        hh, pos = hcv // 8, (hcv % 8) * 128
        vstg = bf(VSTG[ei], 512)
        ei4 = (oc * 4 + ss) % 4
        tf = tern_final if oc == 32 else []
        vehw = bb(VEH[ei4], 512)
        velw = bb(VEL[ei4], 512)
        t_trs = []
        for j in range(4):
            t_trs.append(B.op("tensor", lambda e, bk=bk, j=j, vstg=vstg: e.matmul(
                pb[bk][:, j * 128:(j + 1) * 128], vstg[:, j * 128:(j + 1) * 128],
                idn, start=(j == 0), stop=(j == 3), is_transpose=True),
                waits=[t_cp] if j == 0 else (), inc="sP" if j == 3 else None))
        t_his = None
        for j in range(4):
            t_his = B.op("scalar", lambda e, bk=bk, j=j, vehw=vehw: e.activation(
                vehw[:, j * 128:(j + 1) * 128],
                pb[bk][:, j * 128:(j + 1) * 128], AF.Copy),
                waits=([t_trs[3], veh_free[ei4]] + tf) if j == 0 else (),
                inc="sA" if j == 3 else None)
        t_last_lo = None
        for j in range(4):
            t_last_lo = B.op("vector", lambda e, bk=bk, j=j, vehw=vehw, velw=velw: e.tensor_tensor(
                velw[:, j * 128:(j + 1) * 128],
                pb[bk][:, j * 128:(j + 1) * 128],
                vehw[:, j * 128:(j + 1) * 128], op=AL.subtract),
                waits=([t_trs[3], t_his, vel_free[ei4]] + tf) if j == 0 else (),
                inc="sV" if j == 3 else None)
        dst_hi = vt4[ss * 4:ss * 4 + 4, hh, :, pos:pos + 128].transpose([1, 0, 2])
        dst_lo = vt4[ss * 4:ss * 4 + 4, hh, :, 1024 + pos:1024 + pos + 128].transpose([1, 0, 2])
        veh_free[ei4] = B.dma(dst_hi, vehw, waits=[t_his], stream="vh%d" % ei4)
        vel_free[ei4] = B.dma(dst_lo, velw, waits=[t_last_lo], stream="vl%d" % ei4)
        vstg_free[ei] = t_trs[3]
        bank_free[bk] = t_last_lo

    qk_load(0)
    qk_load(1)
    qk_split(0)
    for oc in range(48):
        whi = bb(WQH[oc % 2], 2048)
        wlo = bb(WQL[oc % 2], 2048)
        t_hi, t_lo = split_tok[oc % 2]
        if oc + 1 < 48:
            qk_split(oc + 1)
        if oc + 2 < 48:
            qk_load(oc + 2)
        t_mm = None
        for ss in range(4):
            bk = ss + 4 * (oc % 2)
            n_mm = 0
            for hc in range(16):
                for (a, bmat) in ((whi, xh), (wlo, xh), (whi, xl)):
                    first = (n_mm == 0)
                    last = (n_mm == 47)
                    t_mm = B.op("tensor", lambda e, a=a, bmat=bmat, hc=hc, ss=ss, bk=bk, first=first, last=last: e.matmul(
                        pb[bk], a[:, hc * 128:(hc + 1) * 128],
                        bmat[:, hc * 2048 + ss * 512: hc * 2048 + ss * 512 + 512],
                        start=first, stop=last),
                        waits=[t_lo, t_hi, bank_free[bk]] if first else (),
                        inc="sP" if last else None)
                    n_mm += 1
            ei = (oc * 4 + ss) % 2
            if oc < 32:
                ehi = bb(EH[ei], 512)
                elo = bb(EL[ei], 512)
                t_ehi = B.op("scalar", lambda e, bk=bk, oc=oc, h=ehi: e.activation(
                    h, pb[bk], AF.Identity, bias=bqkv_sb[:, oc:oc + 1]),
                    waits=[t_mm, t_b1, ev_free[ei]], inc="sA")
                t_elo = B.op("vector", lambda e, bk=bk, oc=oc, h=ehi, l=elo: e.scalar_tensor_tensor(
                    l, pb[bk], bqkv_sb[:, oc:oc + 1], h, op0=AL.add, op1=AL.subtract),
                    waits=[t_mm, t_ehi, ev_free[ei]], inc="sV")
                bank_free[bk] = t_elo
                if oc < 16:
                    r0 = (oc * 4 + ss) * 128
                    B.dma(qf[r0:r0 + 128, 0:512], ehi, waits=[t_ehi],
                          stream="qw%d" % ei)
                    t_w2 = B.dma(qf[r0:r0 + 128, 512:1024], elo, waits=[t_elo],
                                 stream="qw%d" % ei)
                else:
                    hk = oc - 16
                    kh, pos = ss // 2, (ss % 2) * 512
                    r0 = (hk * 2 + kh) * 128
                    B.dma(kf[r0:r0 + 128, pos:pos + 512], ehi, waits=[t_ehi],
                          stream="qw%d" % ei)
                    t_w2 = B.dma(kf[r0:r0 + 128, 1024 + pos:1024 + pos + 512], elo,
                                 waits=[t_elo], stream="qw%d" % ei)
                ev_free[ei] = t_w2
            else:
                hcv = oc - 32
                ei = (oc * 4 + ss) % 2
                vstg = bf(VSTG[ei], 512)
                t_cp = B.op("scalar", lambda e, bk=bk, vstg=vstg: e.copy(
                    vstg, pb[bk]),
                    waits=[t_mm, vstg_free[ei]] +
                    (tern_final if oc == 32 and ss < 2 else []), inc="sA")
                pend = v_pending.pop(0) if v_pending else None
                v_pending.append((oc, ss, bk, ei, t_cp))
                if pend is not None:
                    v_emit(*pend)
            if ss == 3:
                prev_pair[oc % 2] = t_mm
                prev_wqf[oc % 2] = t_lo
            if oc < 32:
                pump(3)
            else:
                if ss == 1 or ss == 3:
                    next(wos_it, None)
    for pend in v_pending:
        v_emit(*pend)
    v_pending = []
    for _ in tern_it:
        pass
    for _ in wos_it:
        pass

    s3_end = [B.stream_last.get(s) for s in
              ("qw0", "qw1", "ww0", "ww1", "vh0", "vh1", "vh2", "vh3",
               "vl0", "vl1", "vl2", "vl3", "tw0", "tw1")] + \
        [B.last_tok["tensor"], B.last_tok["scalar"], B.last_tok["vector"]]

    QSL, SCO, PTH, PTL = 0, 32768, 65536, 81920
    KB = [98304, 102400]
    ESB, PN = 106496, 114688
    VB = [122880, 126976]
    CTH, CTL = 131072, 147456
    WOB = [163840, 167936]
    XRS = 98304
    YB, Y1B = 32768, 0
    DMP = 106496
    Y1FH, Y1FL = 65536, 81920
    QFCS = 131072
    RLS = [163840, 32768]
    T2S = [165888, 34816]
    QPB = [135168, 137216]
    OUTB = [65536, 73728]

    def hsq_off(hc):
        if hc < 16:
            return 98304 + hc * 2048
        if hc < 24:
            return 147456 + (hc - 16) * 2048
        return 167936 + (hc - 24) * 2048

    qsl = bb(QSL, 16384)
    sco = bf(SCO, 8192)
    pth = bb(PTH, 8192)
    ptl = bb(PTL, 8192)
    esb = bf(ESB, 2048)
    pn = bf(PN, 2048)
    cth = bb(CTH, 8192)
    ctl = bb(CTL, 8192)
    xrs = bf(XRS, 8192)
    yb = bf(YB, 8192)
    y1b = bf(Y1B, 8192)
    dmp = bf(DMP, 2048)
    y1fh = bb(Y1FH, 8192)
    y1fl = bb(Y1FL, 8192)
    qfcs = bb(QFCS, 8192)
    rls = [bf(RLS[0], 512), bf(RLS[1], 512)]
    t2s = [bf(T2S[0], 512), bf(T2S[1], 512)]

    grp_end = {k: None for k in ("y2v", "hap", "hlv", "out0", "out1", "pe",
                                 "kbfree")}
    grp_end["y2map"] = {}
    for g in range(4):
        s30 = s3_end if g == 0 else []

        def ge(*keys):
            return [grp_end[k] for k in keys]

        kb_slot_last = [None, None]
        t_kld0 = {}
        for hc in range(2):
            ki = hc % 2
            r0 = (hc * 2 + 0) * 128
            t_kld0[hc] = B.dma(bb(KB[ki], 2048), kf[r0:r0 + 128, :],
                               waits=ge("kbfree") + (s30 if hc == 0 else []),
                               stream="kbg%d" % ki, eng="gpsimd")
        t_qsl = [None] * 16
        hq_order = [h for h in range(16) if h % 4 < 2] + \
            [h for h in range(16) if h % 4 >= 2]
        for idx, hq in enumerate(hq_order):
            r0 = (hq * 4 + g) * 128
            ytok = grp_end["y2map"].get((hq // 4, (hq % 4) // 2, hq % 2))
            t_qsl[hq] = B.dma(qsl[:, hq * 1024:(hq + 1) * 1024], qf[r0:r0 + 128, :],
                              waits=[ytok],
                              stream="qg%d" % hq, eng="gpsimd")
        t_sco_ev = []
        for kh in range(2):
            t_kld = dict(t_kld0) if kh == 0 else {}

            def kb_load(hc, kh=kh):
                ki = (kh * 16 + hc) % 2
                r0 = (hc * 2 + kh) * 128
                t_kld[hc] = B.dma(bb(KB[ki], 2048), kf[r0:r0 + 128, :],
                                  waits=[kb_slot_last[ki]],
                                  stream="kb%d" % ki)

            if kh == 1:
                kb_load(0)
                kb_load(1)
            for hc in range(16):
                ki = (kh * 16 + hc) % 2
                kb = bb(KB[ki], 2048)
                t_hcmm = None
                for qt in range(4):
                    for ks in range(2):
                        bk = qt * 2 + ks
                        qoff = hc * 1024 + qt * 128
                        for ip, (a_off, b_off) in enumerate(((qoff, ks * 512),
                                                            (qoff + 512, ks * 512),
                                                            (qoff, 1024 + ks * 512))):
                            first = (hc == 0 and ip == 0)
                            end_of_acc = (hc == 15 and ip == 2)
                            end_of_kb = (qt == 3 and ks == 1 and ip == 2)
                            w = []
                            if first:
                                w = [bank_free[bk]]
                            if ip == 0 and qt == 0 and ks == 0:
                                w = list(w) + [t_kld[hc], t_qsl[hc]]
                            t_hcmm = B.op("tensor", lambda e, a_off=a_off, b_off=b_off, bk=bk, kb=kb, first=first, last=end_of_acc: e.matmul(
                                pb[bk], qsl[:, a_off:a_off + 128], kb[:, b_off:b_off + 512],
                                start=first, stop=last),
                                waits=w,
                                inc="sP" if (end_of_acc or end_of_kb) else None)
                            if end_of_acc:
                                acc_tok = t_hcmm
                                bank_free[bk] = None
                                t_sco_ev.append((bk, qt, ks, acc_tok))
                kb_slot_last[ki] = t_hcmm
                if hc + 2 < 16:
                    kb_load(hc + 2)
            ev_toks = []
            for (bk, qt, ks, acc_tok) in t_sco_ev[-8:]:
                dst = qt * 2048 + kh * 1024 + ks * 512
                eng = "scalar" if bk % 2 == 0 else "vector"
                exw = ge("hap", "hlv") if (kh == 0 and bk == 0) else []
                if eng == "scalar":
                    tok = B.op(eng, lambda e, bk=bk, dst=dst: e.copy(
                        sco[:, dst:dst + 512], pb[bk]),
                        waits=[acc_tok] + exw, inc="sA")
                else:
                    tok = B.op(eng, lambda e, bk=bk, dst=dst: e.tensor_copy(
                        sco[:, dst:dst + 512], pb[bk]),
                        waits=[acc_tok] + exw, inc="sV")
                bank_free[bk] = tok
                ev_toks.append(tok)
        sco_qt_toks = {qt: [] for qt in range(4)}
        for (bk, qt_, ks_, _a) in t_sco_ev:
            sco_qt_toks[qt_].append(bank_free[bk])
        t_sco_all = [t for t in (ev_toks + [bank_free[i] for i in range(8)])
                     if t is not None]
        t_qsl_lastread = kb_slot_last[1] or kb_slot_last[0]

        t_pt_done = []
        pn2 = [esb, pn]
        pn_read = [None, None]
        pn_norm = [None, None]
        t_tr = None
        for qt in range(4):
            pnb = pn2[qt % 2]
            t_nm = B.op("vector", lambda e, qt=qt: e.tensor_reduce(
                vcol(NMAX), sco[:, qt * 2048:(qt + 1) * 2048], axis=AX.X, op=AL.max,
                negate=True), waits=sco_qt_toks[qt] + [pn_norm[qt % 2]],
                inc="sV")
            t_eb = B.op("vector", lambda e: e.tensor_tensor(
                vcol(EBIAS), vcol(NMAX), vcol(SSCB), op=AL.mult),
                waits=[t_sscb], inc="sV")
            t_exp = B.op("scalar", lambda e, qt=qt, pnb=pnb: e.activation(
                pnb, sco[:, qt * 2048:(qt + 1) * 2048], AF.Exp,
                bias=vcol(EBIAS), scale=vcol(SSCB), accum_out=vcol(ESUM)),
                waits=[t_eb, pn_read[qt % 2]] + (ge("pe") if qt == 0 else []),
                inc="sA")
            t_rq = B.op("vector", lambda e: e.reciprocal(vcol(RQ), vcol(ESUM)),
                        waits=[t_exp], inc="sV")
            t_pn = B.op("vector", lambda e, pnb=pnb: e.tensor_single_scalar(
                pnb, pnb, vcol(RQ), op=AL.mult),
                waits=[t_rq], inc="sV")
            pn_norm[qt % 2] = t_pn
            for kc4 in range(4):
                bk = kc4 % 2
                t_tr = None
                for j in range(4):
                    kc = kc4 * 4 + j
                    t_tr = B.op("tensor", lambda e, kc=kc, j=j, bk=bk, pnb=pnb: e.matmul(
                        pb[bk][:, j * 128:(j + 1) * 128],
                        pnb[:, kc * 128:(kc + 1) * 128], idn,
                        start=(j == 0), stop=(j == 3), is_transpose=True),
                        waits=[t_pn, bank_free[bk]] if j == 0 else (),
                        inc="sP" if j == 3 else None)
                dsth = pth.rearrange("p (kc c) -> p kc c", c=512)[
                    :, kc4 * 4:kc4 * 4 + 4, qt * 128:qt * 128 + 128]
                dstl = ptl.rearrange("p (kc c) -> p kc c", c=512)[
                    :, kc4 * 4:kc4 * 4 + 4, qt * 128:qt * 128 + 128]
                t_hi = B.op("scalar", lambda e, bk=bk, dsth=dsth: e.activation(
                    dsth, pb[bk], AF.Copy),
                    waits=[t_tr] + (ge("out0", "out1") if qt == 0 and kc4 == 0 else []),
                    inc="sA")
                t_lo = B.op("vector", lambda e, bk=bk, dsth=dsth, dstl=dstl: e.tensor_tensor(
                    dstl, pb[bk], dsth, op=AL.subtract),
                    waits=[t_tr, t_hi], inc="sV")
                bank_free[bk] = t_lo
                t_pt_done.append(t_lo)
            pn_read[qt % 2] = t_tr
        t_sco_lastread = t_exp
        prev_pn_read = t_tr

        t_ct_done = []
        vb_slot_last = [None, None]
        for hh in range(2):
            t_vld = {}

            def vb_load(kc, hh=hh):
                vi = (hh * 16 + kc) % 2
                r0 = (kc * 2 + hh) * 128
                t_vld[kc] = B.dma(bb(VB[vi], 2048), vt[r0:r0 + 128, :],
                                  waits=[vb_slot_last[vi]] +
                                  (ge("pe") if hh == 0 and kc < 2 else []),
                                  stream="vb%d" % vi)

            vb_load(0)
            vb_load(1)
            t_kcmm = {}
            for kc in range(16):
                vi = (hh * 16 + kc) % 2
                vbuf = bb(VB[vi], 2048)
                t_last = None
                for h8 in range(8):
                    bk = h8
                    for ip in range(3):
                        a_off = h8 * 128 if ip != 1 else 1024 + h8 * 128
                        pmat = pth if ip != 2 else ptl
                        first = (kc == 0 and ip == 0)
                        end_of_acc = (kc == 15 and ip == 2)
                        end_of_vb = (h8 == 7 and ip == 2)
                        w = []
                        if first:
                            w = [t_pt_done[-1], bank_free[bk]]
                        if ip == 0 and h8 == 0:
                            w = list(w) + [t_vld[kc]]
                        t_last = B.op("tensor", lambda e, vbuf=vbuf, a_off=a_off, kc=kc, pmat=pmat, bk=bk, first=first, last=end_of_acc: e.matmul(
                            pb[bk], vbuf[:, a_off:a_off + 128],
                            pmat[:, kc * 512:kc * 512 + 512],
                            start=first, stop=last),
                            waits=w, inc="sP" if (end_of_acc or end_of_vb) else None)
                        if end_of_acc:
                            t_kcmm[bk] = t_last
                vb_slot_last[vi] = t_last
                if kc + 2 < 16:
                    vb_load(kc + 2)
            for h8 in range(8):
                hca = hh * 8 + h8
                dst = hca * 512
                t_hi = B.op("scalar", lambda e, bk=h8, dst=dst, hca=hca: e.activation(
                    cth[:, dst:dst + 512], pb[bk], AF.Identity,
                    bias=bqkv_sb[:, 32 + hca:33 + hca]),
                    waits=[t_kcmm[h8]], inc="sA")
                t_lo = B.op("vector", lambda e, bk=h8, dst=dst, hca=hca: e.scalar_tensor_tensor(
                    ctl[:, dst:dst + 512], pb[bk], bqkv_sb[:, 32 + hca:33 + hca],
                    cth[:, dst:dst + 512], op0=AL.add, op1=AL.subtract),
                    waits=[t_kcmm[h8], t_hi], inc="sV")
                bank_free[h8] = t_lo
                t_ct_done.append(t_lo)
        t_pt_lastread = vb_slot_last[1] or vb_slot_last[0]

        t_wld = {0: {}, 1: {}}
        wob_slot_last = [None, None]

        def wob_load(hc, oh):
            wi = (oh * 16 + hc) % 2
            r0 = (hc * 2 + oh) * 128
            t_wld[oh][hc] = B.dma(bb(WOB[wi], 2048), wos[r0:r0 + 128, :],
                                  waits=[wob_slot_last[wi]] +
                                  (ge("pe", "y2v") if oh == 0 and hc < 2 else []),
                                  stream="wo%d" % wi)

        wob_load(0, 0)
        wob_load(1, 0)
        t_xld = []
        for sci in range(4):
            t_xld.append(B.dma(xrs[:, sci * 2048:(sci + 1) * 2048],
                               x_in[g * 512 + sci * 128: g * 512 + sci * 128 + 128, :],
                               waits=[t_pt_lastread if sci == 0 else None,
                                      prev_pn_read if sci == 0 else None,
                                      t_ct_done[-1] if sci == 0 else None],
                               stream="xs"))
        t_y_done = []
        y_map = {}
        t_d_lastmm = None
        for oh in range(2):
            if oh == 1:
                wob_load(0, 1)
                wob_load(1, 1)
            for hc in range(16):
                wi = (oh * 16 + hc) % 2
                wbuf = bb(WOB[wi], 2048)
                t_last = None
                for sci in range(4):
                    for os2 in range(2):
                        bk = sci * 2 + os2
                        coff = hc * 512 + sci * 128
                        for ip in range(3):
                            amat = cth if ip != 1 else ctl
                            b_off = os2 * 512 if ip != 2 else 1024 + os2 * 512
                            first = (hc == 0 and ip == 0)
                            end_of_wb = (sci == 3 and os2 == 1 and ip == 2)
                            w = []
                            if first:
                                w = [t_ct_done[-1], bank_free[bk]]
                            if ip == 0 and sci == 0 and os2 == 0:
                                w = list(w) + [t_wld[oh][hc]]
                            t_last = B.op("tensor", lambda e, amat=amat, coff=coff, wbuf=wbuf, b_off=b_off, bk=bk, first=first: e.matmul(
                                pb[bk], amat[:, coff:coff + 128], wbuf[:, b_off:b_off + 512],
                                start=first, stop=False),
                                waits=w, inc="sP" if end_of_wb else None)
                wob_slot_last[wi] = t_last
                t_d_lastmm = t_last
                if hc + 2 < 16:
                    wob_load(hc + 2, oh)
            for bk in range(8):
                sci, os2 = bk // 2, bk % 2
                boff = oh * 1024 + os2 * 512
                B.op("tensor", lambda e, bk=bk, boff=boff: e.matmul(
                    pb[bk], ones_bf, bout_hi[:, boff:boff + 512],
                    start=False, stop=False), waits=[t_bh, t_g0] if g == 0 and oh == 0 and bk == 0 else ())
                t_bias = B.op("tensor", lambda e, bk=bk, boff=boff: e.matmul(
                    pb[bk], ones_bf, bout_lo[:, boff:boff + 512],
                    start=False, stop=True), waits=[t_bl] if g == 0 and oh == 0 and bk == 0 else (), inc="sP")
                dst = sci * 2048 + oh * 1024 + os2 * 512
                t_y = B.op("vector", lambda e, bk=bk, dst=dst: e.tensor_tensor(
                    yb[:, dst:dst + 512], pb[bk], xrs[:, dst:dst + 512], op=AL.add),
                    waits=[t_bias, t_xld[-1],
                           t_sco_lastread if oh == 0 and bk == 0 else None], inc="sV")
                bank_free[bk] = t_y
                y_map[(sci, oh, os2)] = t_y
                t_y_done.append(t_y)

        def layer_norm(src_view, dst_view, waits_first, sci, dump_extra=None):
            base = 64 + sci * 9
            NS, NMc, SSQc, T1c, T2c = base, base + 1, base + 2, base + 3, base + 4
            Vc, SDc, RSc, NMRc = base + 5, base + 6, base + 7, base + 8
            t_ns = B.op("vector", lambda e: e.tensor_reduce(
                vcol(NS), src_view, axis=AX.X, op=AL.add, negate=True),
                waits=waits_first, inc="sV")
            B.op("vector", lambda e: e.tensor_single_scalar(
                vcol(NMc), vcol(NS), INV2048, op=AL.mult))
            t_sq_ = B.op("scalar", lambda e: e.activation(
                dmp, src_view, AF.Square, accum_out=vcol(SSQc)),
                waits=list(waits_first) + [dump_extra], inc="sA")
            B.op("vector", lambda e: e.tensor_single_scalar(
                vcol(T1c), vcol(SSQc), INV2048, op=AL.mult), waits=[t_sq_])
            B.op("vector", lambda e: e.tensor_tensor(vcol(T2c), vcol(NMc), vcol(NMc),
                                                     op=AL.mult))
            t_var = B.op("vector", lambda e: e.tensor_tensor(
                vcol(Vc), vcol(T1c), vcol(T2c), op=AL.subtract), inc="sV")
            t_sd = B.op("scalar", lambda e: e.activation(
                vcol(SDc), vcol(Vc), AF.Sqrt, bias=vcol(EPSB)),
                waits=[t_var, t_epsb], inc="sA")
            t_rs = B.op("vector", lambda e: e.reciprocal(vcol(RSc), vcol(SDc)),
                        waits=[t_sd], inc="sV")
            t_nmr = B.op("vector", lambda e: e.tensor_tensor(
                vcol(NMRc), vcol(NMc), vcol(RSc), op=AL.mult), waits=[t_rs], inc="sV")
            t_ap = B.op("scalar", lambda e: e.activation(
                dst_view, src_view, AF.Identity, bias=vcol(NMRc), scale=vcol(RSc)),
                waits=[t_nmr, t_rs], inc="sA")
            return t_ap, t_nmr

        t_y1_tr = []
        t_ln_prev = None
        ln1_vtok = [None] * 4
        for sci in range(4):
            y_toks = [y_map[(sci, oh_, os_)] for oh_ in range(2) for os_ in range(2)]
            t_ap, _ = layer_norm(
                yb[:, sci * 2048:(sci + 1) * 2048],
                y1b[:, sci * 2048:(sci + 1) * 2048],
                y_toks + [t_qsl_lastread if sci == 0 else None],
                sci, dump_extra=t_y_done[-1] if sci == 0 else None)
            ln1_vtok[sci] = _
            t_ln_prev = t_ap
            for dc4 in range(4):
                bk = (sci * 4 + dc4) % 2
                t_tr = None
                for j in range(4):
                    src_off = sci * 2048 + (dc4 * 4 + j) * 128
                    t_tr = B.op("tensor", lambda e, src_off=src_off, j=j, bk=bk: e.matmul(
                        pb[bk][:, j * 128:(j + 1) * 128],
                        y1b[:, src_off:src_off + 128], idn,
                        start=(j == 0), stop=(j == 3), is_transpose=True),
                        waits=[t_ap, bank_free[bk]] if j == 0 else (),
                        inc="sP" if j == 3 else None)
                dsth = y1fh.rearrange("p (dc c) -> p dc c", c=512)[
                    :, dc4 * 4:dc4 * 4 + 4, sci * 128:sci * 128 + 128]
                dstl = y1fl.rearrange("p (dc c) -> p dc c", c=512)[
                    :, dc4 * 4:dc4 * 4 + 4, sci * 128:sci * 128 + 128]
                t_hi = B.op("scalar", lambda e, bk=bk, dsth=dsth: e.activation(
                    dsth, pb[bk], AF.Copy), waits=[t_tr], inc="sA")
                t_lo = B.op("vector", lambda e, bk=bk, dsth=dsth, dstl=dstl: e.tensor_tensor(
                    dstl, pb[bk], dsth, op=AL.subtract),
                    waits=[t_tr, t_hi], inc="sV")
                bank_free[bk] = t_lo
                t_y1_tr.append(t_lo)

        t_hsq_done = []
        t_qf_free = None
        t_rl_free = [None, None]
        t_t2_free = [None, None]
        for hcb in range(10):
            t_qld = []
            for dc in range(16):
                t_qld.append(B.dma(qfcs[:, dc * 512:(dc + 1) * 512],
                             qfc[dc * 128:(dc + 1) * 128, hcb * 512:(hcb + 1) * 512],
                             waits=[t_qf_free if dc == 0 else None,
                                    t_d_lastmm if hcb == 0 and dc == 0 else None] +
                             (tern_done[0] if g == 0 and hcb == 0 and dc == 0 else []),
                             stream="ts%d" % dc))
            t_mmf = {}
            for h4 in range(4):
                bk = h4 + 4 * (hcb % 2)
                n_mm = 0
                for dc in range(16):
                    for amat in (y1fh, y1fl):
                        first = (n_mm == 0)
                        last = (n_mm == 31)
                        w = []
                        if first:
                            w = [t_y1_tr[-1], bank_free[bk]]
                        if h4 == 0 and amat is y1fh:
                            w = list(w) + [t_qld[dc]]
                        t_mmf[bk] = B.op("tensor", lambda e, dc=dc, h4=h4, amat=amat, bk=bk, first=first, last=last: e.matmul(
                            pb[bk], qfcs[:, dc * 512 + h4 * 128: dc * 512 + h4 * 128 + 128],
                            amat[:, dc * 512:(dc + 1) * 512],
                            start=first, stop=last),
                            waits=w, inc="sP" if last else None)
                        n_mm += 1
                hca = hcb * 4 + h4
                hoff = hsq_off(hca)
                pi = hca % 2
                rl = rls[pi]
                t2 = t2s[pi]
                t_rl = B.op("scalar", lambda e, bk=bk, rl=rl: e.activation(
                    rl, pb[bk], AF.Relu, scale=vcol(SFCB)),
                    waits=[t_mmf[bk], tern_scale[0], t_rl_free[pi]] +
                    ([ln1_vtok[0]] if hca < 2 else []), inc="sA")
                t_t2 = B.op("vector", lambda e, rl=rl, t2=t2: e.tensor_tensor(
                    t2, rl, rl, op=AL.mult),
                    waits=[t_rl, t_t2_free[pi]], inc="sV")
                t_hi = B.op("scalar", lambda e, hoff=hoff, t2=t2: e.activation(
                    bb(hoff, 512), t2, AF.Copy), waits=[t_t2], inc="sA")
                t_lo = B.op("vector", lambda e, hoff=hoff, t2=t2: e.tensor_tensor(
                    bb(hoff + 1024, 512), t2, bb(hoff, 512), op=AL.subtract),
                    waits=[t_hi], inc="sV")
                bank_free[bk] = t_rl
                t_rl_free[pi] = t_t2
                t_t2_free[pi] = t_lo
                t_hsq_done.append(t_lo)
            t_qf_free = t_mmf[3 + 4 * (hcb % 2)]

        t_y2_done = []
        g_y2map = {}
        g_kbfree_new = [None]
        qp_slot_last = [t_qf_free, t_qf_free]
        t_mmp2 = {}
        for dh in range(2):
            t_qpld = {}

            def qp_load(hc, dh=dh):
                qi = (dh * FC + hc) % 2
                t_qpld[hc] = B.dma(bb(QPB[qi], 1024),
                                   qpr[hc * 128:(hc + 1) * 128, dh * 1024:(dh + 1) * 1024],
                                   waits=[qp_slot_last[qi]] +
                                   (tern_done[1] if g == 0 and dh == 0 and hc < 2 else []),
                                   stream="qp%d" % qi)

            qp_load(0)
            qp_load(1)
            for hc in range(FC):
                qi = (dh * FC + hc) % 2
                qpb = bb(QPB[qi], 1024)
                hoff = hsq_off(hc)
                t_last = None
                for sci in range(4):
                    for ds in range(2):
                        bk = sci * 2 + ds
                        for ip in range(2):
                            a_ap = bb(hoff + (0 if ip == 0 else 1024), 512)
                            first = (hc == 0 and ip == 0)
                            end_of_acc = (hc == FC - 1 and ip == 1)
                            end_of_qp = (sci == 3 and ds == 1 and ip == 1)
                            w = []
                            if first:
                                w = [t_hsq_done[-1], bank_free[bk]]
                            if ip == 0 and sci == 0 and ds == 0:
                                w = list(w) + [t_qpld[hc]]
                            t_last = B.op("tensor", lambda e, a_ap=a_ap, sci=sci, qpb=qpb, ds=ds, bk=bk, first=first, last=end_of_acc: e.matmul(
                                pb[bk], a_ap[:, sci * 128:sci * 128 + 128],
                                qpb[:, ds * 512:ds * 512 + 512],
                                start=first, stop=last),
                                waits=w, inc="sP" if (end_of_acc or end_of_qp) else None)
                            if end_of_acc:
                                t_mmp2[bk] = t_last
                qp_slot_last[qi] = t_last
                if dh == 1 and hc == 3:
                    g_kbfree_new[0] = t_last
                if hc + 2 < FC:
                    qp_load(hc + 2)
            for bk in range(8):
                sci, ds = bk // 2, bk % 2
                dst = sci * 2048 + dh * 1024 + ds * 512
                t_y2 = B.op("vector", lambda e, bk=bk, dst=dst: e.scalar_tensor_tensor(
                    yb[:, dst:dst + 512], pb[bk], vcol(SPRB), y1b[:, dst:dst + 512],
                    op0=AL.mult, op1=AL.add),
                    waits=[t_mmp2[bk], tern_scale[1]], inc="sV")
                bank_free[bk] = t_y2
                g_y2map[(sci, dh, ds)] = t_y2
                t_y2_done.append(t_y2)

        t_ln_prev = None
        t_ln_v = None
        outb_free = [None, None]
        t_out_wr = None
        for sci in range(4):
            ob = bf(OUTB[sci % 2], 2048)
            y2_toks = [g_y2map[(sci, dh_, ds_)] for dh_ in range(2) for ds_ in range(2)]
            t_ap, t_ln_v = layer_norm(
                yb[:, sci * 2048:(sci + 1) * 2048], ob,
                y2_toks + [outb_free[sci % 2]],
                sci)
            t_ln_prev = t_ap
            t_out_wr = B.dma(out_hbm[g * 512 + sci * 128: g * 512 + sci * 128 + 128, :],
                             ob, waits=[t_ap], stream="ou%d" % (sci % 2))
            outb_free[sci % 2] = t_out_wr
        grp_end = {"y2v": t_y2_done[-1], "hap": t_ln_prev, "hlv": t_ln_v,
                   "out0": outb_free[0], "out1": outb_free[1],
                   "pe": t_mmp2[7], "y2map": g_y2map, "kbfree": g_kbfree_new[0]}

    assert nc.sbuf_base <= my_base, (nc.sbuf_base, my_base)
    B.op("sync", lambda e: e.nop(),
         waits=[("D_ou0", B.plan["D_ou0"], "x"), ("D_ou1", B.plan["D_ou1"], "x")])
    B.finalize()
    return nc


_NC_CACHE = []


def kernel(L0_attn_hidden_states, L0_attn_w_qkv, L0_attn_b_qkv, L0_attn_w_out,
           L0_attn_b_out, L0_ffn_w_fc, L0_ffn_w_proj):
    x = np.asarray(L0_attn_hidden_states, dtype=np.float32)
    wqkv = np.asarray(L0_attn_w_qkv, dtype=np.float32)
    bqkv = np.asarray(L0_attn_b_qkv, dtype=np.float32)
    wout = np.asarray(L0_attn_w_out, dtype=np.float32)
    bout = np.asarray(L0_attn_b_out, dtype=np.float32)
    wfc = np.asarray(L0_ffn_w_fc, dtype=np.float32)
    wproj = np.asarray(L0_ffn_w_proj, dtype=np.float32)

    w1 = wqkv.reshape(48, 128, 16, 128)
    wqk_t = np.ascontiguousarray(w1.transpose(0, 3, 2, 1)).reshape(48 * 128, 2048)
    bqkv_t = np.ascontiguousarray(bqkv.reshape(48, 128).T)
    bout_r = np.ascontiguousarray(bout.reshape(1, 2048))
    wot_t = np.ascontiguousarray(wout.T).reshape(16 * 128, 2048)
    wfct_t = np.ascontiguousarray(wfc.T).reshape(16 * 128, 5120)
    wprt_t = np.ascontiguousarray(wproj.T).reshape(40 * 128, 2048)

    if not _NC_CACHE:
        _NC_CACHE.append(build_core())
    nc = _NC_CACHE[0]
    shared = dict(wqk=wqk_t, bqkv=bqkv_t, bout=bout_r,
                  wot=wot_t, wfct=wfct_t, wprt=wprt_t)
    in_maps = [dict(shared, x=np.ascontiguousarray(x[i])) for i in range(8)]
    res = run_bass_kernel_spmd(nc, in_maps, core_ids=list(range(8)))
    return np.stack([np.asarray(res.results[i]["out"]) for i in range(8)], axis=0)


# version 28
# speedup vs baseline: 1.0169x; 1.0169x over previous
import numpy as np
import concourse.bass as bass
import concourse.mybir as mybir
from concourse.bass_utils import run_bass_kernel_spmd

F32 = mybir.dt.float32
BF16 = mybir.dt.bfloat16
AL = mybir.AluOpType
AF = mybir.ActivationFunctionType
AX = mybir.AxisListType

ENGINES = ("sync", "gpsimd", "scalar", "vector", "tensor")

S = 2048
H = 2048
FC = 40
INV2048 = 1.0 / 2048.0


class Builder:
    def __init__(self, nc):
        self.nc = nc
        self.ops = {e: [] for e in ENGINES}
        self.plan = {}
        self.sem_names = []
        self.last_tok = {e: None for e in ENGINES}

    def sem(self, name):
        if name not in self.plan:
            self.plan[name] = 0
            self.sem_names.append(name)

    ENGSEM = {"vector": "sV", "scalar": "sA", "gpsimd": "sG"}

    def op(self, eng, fn, waits=(), inc=None, n=1):
        self_wait = None
        if eng in self.ENGSEM:
            sname = self.ENGSEM[eng]
            self.sem(sname)
            prev = self.plan[sname]
            if prev > 0:
                self_wait = (sname, prev, "_self")
            if inc is None:
                inc = sname
        tok = None
        if inc is not None:
            self.sem(inc)
            self.plan[inc] += n
            tok = (inc, self.plan[inc], eng)
            self.last_tok[eng] = tok
        ws = []
        for w in list(waits) + [self_wait]:
            if w is None:
                continue
            sname, v, peng = w
            ws.append((sname, v))
        self.ops[eng].append((fn, ws, (inc, n) if inc is not None else None))
        return tok

    def dma(self, out, in_, waits=(), stream="dL", eng="sync"):
        tok = self.op(eng, lambda e: e.dma_start(out=out, in_=in_),
                      waits=waits, inc="D_" + stream, n=16)
        if not hasattr(self, "stream_last"):
            self.stream_last = {}
        self.stream_last[stream] = tok
        return tok

    def finalize(self):
        nc = self.nc
        import contextlib
        stack = contextlib.ExitStack()
        handles = {}
        for name in self.sem_names:
            handles[name] = stack.enter_context(nc.semaphore(name))
        with stack:
            with nc.Block() as block:
                def make(eng_name):
                    def run(eng):
                        satisfied = {}
                        for fn, ws, inc in self.ops[eng_name]:
                            for sname, v in ws:
                                if satisfied.get(sname, -1) >= v:
                                    continue
                                eng.wait_ge(handles[sname], v)
                                satisfied[sname] = v
                            ins = fn(eng)
                            if inc is not None:
                                ins.then_inc(handles[inc[0]], inc[1])
                    return run
                for eng_name in ENGINES:
                    getattr(block, eng_name)(make(eng_name))


def build_core():
    nc = bass.Bass("TRN2", target_bir_lowering=False)

    x_in = nc.dram_tensor("x", [S, H], F32, kind="ExternalInput").ap()
    wqk = nc.dram_tensor("wqk", [48 * 128, 2048], F32, kind="ExternalInput").ap()
    bqkv_in = nc.dram_tensor("bqkv", [128, 48], F32, kind="ExternalInput").ap()
    bout_in = nc.dram_tensor("bout", [1, 2048], F32, kind="ExternalInput").ap()
    wot = nc.dram_tensor("wot", [16 * 128, 2048], F32, kind="ExternalInput").ap()
    wfct = nc.dram_tensor("wfct", [16 * 128, 5120], F32, kind="ExternalInput").ap()
    wprt = nc.dram_tensor("wprt", [40 * 128, 2048], F32, kind="ExternalInput").ap()
    out_hbm = nc.dram_tensor("out", [S, H], F32, kind="ExternalOutput").ap()

    qf = nc.dram_tensor("qf", [16 * 4 * 128, 1024], BF16).ap()
    kf = nc.dram_tensor("kf", [16 * 2 * 128, 2048], BF16).ap()
    vt = nc.dram_tensor("vt", [16 * 2 * 128, 2048], BF16).ap()
    wos = nc.dram_tensor("wos", [16 * 2 * 128, 2048], BF16).ap()
    qfc = nc.dram_tensor("qfc", [16 * 128, 5120], BF16).ap()
    qpr = nc.dram_tensor("qpr", [40 * 128, 2048], BF16).ap()

    AR_B = 200704
    my_base = (nc.sbuf_top - AR_B) & ~31
    _bufs = {}

    def _at(off, n, dt, tag):
        key = (off, n, tag)
        if key not in _bufs:
            _bufs[key] = nc.alloc_sbuf_tensor_at(
                "%s_%d_%d" % (tag, off, n), [128, n], dt,
                offset=my_base + off).ap()
        return _bufs[key]

    def bb(off, n):
        assert off % 2 == 0 and off + 2 * n <= AR_B, (off, n)
        return _at(off, n, BF16, "ab")

    def bf(off, n):
        assert off % 4 == 0 and off + 4 * n <= AR_B, (off, n)
        return _at(off, n, F32, "af")

    idn = nc.alloc_sbuf_tensor("idn", [128, 128], F32).ap()
    ones_row = nc.alloc_sbuf_tensor("ones_row", [1, 128], F32).ap()
    ones_bf = nc.alloc_sbuf_tensor("ones_bf", [1, 128], BF16).ap()
    bout_hi = nc.alloc_sbuf_tensor("bout_hi", [1, 2048], BF16).ap()
    bout_lo = nc.alloc_sbuf_tensor("bout_lo", [1, 2048], BF16).ap()
    bqkv_sb = nc.alloc_sbuf_tensor("bqkv_sb", [128, 48], F32).ap()
    st = nc.alloc_sbuf_tensor("st", [128, 128], F32).ap()

    pb = [nc.alloc_psum_tensor("pb%d" % i, [128, 512], F32).ap() for i in range(8)]

    B = Builder(nc)
    for s_ in ("dL", "dS", "dO", "sP", "sA", "sV", "sG"):
        B.sem(s_)

    ONESC, SSCB, EPSB = 0, 1, 2
    THRF, NTHRF, THRP, NTHRP, SFCB, SPRB = 3, 4, 5, 6, 7, 8
    NMAX, EBIAS, ESUM, RQ = 9, 10, 11, 12
    NSUM, NM, SSQ, T1, T2M, VAR, SD, RSTD, NMR = 13, 14, 15, 16, 17, 18, 19, 20, 21
    TMP0, ACC0 = 22, 23
    C2048, CN, C7, C10, C1E5, CZERO = 40, 41, 42, 43, 44, 45
    SQ2048, SSC, C07, EPS = 46, 47, 48, 49
    TOTF, MEANF, AMF, THRFs, NTHRFs = 50, 51, 52, 53, 54
    TOTP, MEANP, AMP, THRPs, NTHRPs = 55, 56, 57, 58, 59
    TMP1, RN = 60, 61

    def cell(c):
        return st[0:1, c:c + 1]

    def vcol(c):
        return st[:, c:c + 1]

    t_mi = B.op("gpsimd", lambda e: e.memset(idn, 0.0), inc="sG")
    B.op("gpsimd", lambda e: e.affine_select(
        out=idn, in_=idn, compare_op=AL.not_equal, fill=1.0, base=0,
        pattern=[[-1, 128]], channel_multiplier=1), waits=[t_mi], inc="sG")
    B.op("gpsimd", lambda e: e.memset(ones_row, 1.0), inc="sG")
    B.op("gpsimd", lambda e: e.memset(vcol(ONESC), 1.0), inc="sG")
    for c, v in ((C2048, 2048.0), (CN, 10485760.0), (C7, 7.0), (C10, 10.0),
                 (C1E5, 100000.0), (CZERO, 0.0)):
        B.op("gpsimd", lambda e, c=c, v=v: e.memset(cell(c), v), inc="sG")
    t_g0 = ("sG", B.plan["sG"], "gpsimd")

    t_b1 = B.dma(bqkv_sb, bqkv_in[:, :], stream="b0")
    bout_stage = bf(131072, 2048)[0:1, :]
    t_b2 = B.dma(bout_stage, bout_in[:, :], stream="b1")
    B.op("gpsimd", lambda e: e.memset(ones_bf, 1.0), inc="sG")
    t_bh = B.op("scalar", lambda e: e.activation(bout_hi, bout_stage, AF.Copy),
                waits=[t_b2], inc="sA")
    t_bl = B.op("vector", lambda e: e.tensor_tensor(bout_lo, bout_stage, bout_hi,
                                                    op=AL.subtract),
                waits=[t_b2, t_bh], inc="sV")

    t_sq0 = B.op("scalar", lambda e: e.activation(cell(SQ2048), cell(C2048), AF.Sqrt),
                 waits=[t_g0], inc="sA")
    t_ssc = B.op("vector", lambda e: e.reciprocal(cell(SSC), cell(SQ2048)),
                 waits=[t_sq0, t_g0], inc="sV")
    t_r10 = B.op("vector", lambda e: e.reciprocal(cell(TMP1), cell(C10)), waits=[t_g0],
                 inc="sV")
    t_c07 = B.op("vector", lambda e: e.tensor_tensor(cell(C07), cell(C7), cell(TMP1),
                                                     op=AL.mult), waits=[t_r10], inc="sV")
    t_rn = B.op("vector", lambda e: e.reciprocal(cell(RN), cell(CN)), inc="sV")
    t_eps = B.op("vector", lambda e: e.reciprocal(cell(EPS), cell(C1E5)), inc="sV")

    bank_last = {i: None for i in range(8)}

    def bcast(src_cell, dst_col, bank, waits):
        t_mm = B.op("tensor", lambda e, s=src_cell, b=bank: e.matmul(
            pb[b][:, 0:1], ones_row, s, start=True, stop=True),
            waits=list(waits) + [bank_last[bank], t_g0], inc="sP")
        t_cp = B.op("vector", lambda e, d=dst_col, b=bank: e.tensor_copy(
            vcol(d), pb[b][:, 0:1]), waits=[t_mm], inc="sV")
        bank_last[bank] = t_cp
        return t_cp

    t_sscb = bcast(cell(SSC), SSCB, 0, [t_ssc])
    t_epsb = bcast(cell(EPS), EPSB, 1, [t_eps])

    TZS = 167936
    tz_wf = [bf(TZS + k * 4096, 1024) for k in range(4)]
    tz_ta = bf(TZS + 16384, 1024)
    tz_tb = bf(TZS + 20480, 1024)
    tz_q = [bb(TZS + 24576, 1024), bb(TZS + 26624, 1024)]

    tern_done = {}
    tern_scale = {}
    tern_final = []

    def bcast_b(src_cell, dst_col, bank_free, bk, waits):
        t_mm = B.op("tensor", lambda e, s=src_cell, b=bk: e.matmul(
            pb[b][:, 0:1], ones_row, s, start=True, stop=True),
            waits=list(waits) + [bank_free[bk]], inc="sP")
        t_cp = B.op("vector", lambda e, d=dst_col, b=bk: e.tensor_copy(
            vcol(d), pb[b][:, 0:1]), waits=[t_mm], inc="sV")
        bank_free[bk] = t_cp
        return t_cp

    def tern_units(bank_free):
        P1F = [(wfct, i, t0) for i in range(16) for t0 in range(0, 5120, 1024)]
        P1P = [(wprt, i, t0) for i in range(40) for t0 in range(0, 2048, 1024)]
        sched = P1F + P1P + P1F + P1P
        n1 = len(P1F) + len(P1P)
        consumer_tok = [None] * 4
        load_tok = [None] * len(sched)
        n_loaded = [0]

        def issue_load(k):
            sl = k % 4
            wsrc, i, t0 = sched[k]
            load_tok[k] = B.dma(tz_wf[sl], wsrc[i * 128:(i + 1) * 128, t0:t0 + 1024],
                                waits=[consumer_tok[sl]], stream="tzs%d" % sl,
                                eng="gpsimd")
            n_loaded[0] += 1

        for k in range(min(4, len(sched))):
            issue_load(k)

        t_mz = B.op("gpsimd", lambda e: e.memset(vcol(ACC0), 0.0),
                    waits=[t_g0], inc="sG")
        stats = {}
        t_a = None
        t_q = None
        q_free = [None, None]
        for k in range(len(sched)):
            sl = k % 4
            buf = tz_wf[sl]
            if k < n1:
                t_ab = B.op("scalar", lambda e, b=buf: e.activation(
                    tz_ta, b, AF.Abs, accum_out=vcol(TMP0)),
                    waits=[load_tok[k], t_mz, t_a], inc="sA")
                t_a = B.op("vector", lambda e: e.tensor_tensor(
                    vcol(ACC0), vcol(ACC0), vcol(TMP0), op=AL.add),
                    waits=[t_ab], inc="sV")
                consumer_tok[sl] = t_ab
            else:
                kk = k - n1
                wi = 0 if kk < len(P1F) else 1
                wsrc, i, t0 = sched[k]
                dst = qfc if wi == 0 else qpr
                THRv, NTHRv = (THRF, NTHRF) if wi == 0 else (THRP, NTHRP)
                qsl_ = (k % 2)
                qb = tz_q[qsl_]
                t_s1 = B.op("scalar", lambda e, b=buf, tc=NTHRv: e.activation(
                    tz_ta, b, AF.Sign, bias=vcol(tc)),
                    waits=[load_tok[k], t_q] + (stats[wi] if (kk == 0 or kk == len(P1F)) else []),
                    inc="sA")
                t_s2 = B.op("scalar", lambda e, b=buf, tc=THRv: e.activation(
                    tz_tb, b, AF.Sign, bias=vcol(tc)), inc="sA")
                t_q = B.op("vector", lambda e, qb=qb: e.tensor_tensor(
                    qb, tz_ta, tz_tb, op=AL.add),
                    waits=[q_free[qsl_], t_s2], inc="sV")
                q_free[qsl_] = B.dma(dst[i * 128:(i + 1) * 128, t0:t0 + 1024], qb,
                                     waits=[t_q], stream="tw%d" % qsl_, eng="gpsimd")
                consumer_tok[sl] = t_s2
                if kk == len(P1F) - 1:
                    tern_done[0] = [B.stream_last.get("tw0"), B.stream_last.get("tw1")]
            if n_loaded[0] < len(sched):
                issue_load(n_loaded[0])
            if k == len(P1F) - 1 or k == n1 - 1:
                wi = 0 if k == len(P1F) - 1 else 1
                TOT, MEAN, AM, THRc, NTHRc = ((TOTF, MEANF, AMF, THRFs, NTHRFs) if wi == 0
                                              else (TOTP, MEANP, AMP, THRPs, NTHRPs))
                THRv, NTHRv, Sv = ((THRF, NTHRF, SFCB) if wi == 0 else (THRP, NTHRP, SPRB))
                t_tot = B.op("gpsimd", lambda e, TOT=TOT: e.tensor_reduce(
                    cell(TOT), vcol(ACC0), axis=AX.C, op=AL.add),
                    waits=[t_a], inc="sG")
                t_m = B.op("vector", lambda e, TOT=TOT, MEAN=MEAN: e.tensor_tensor(
                    cell(MEAN), cell(TOT), cell(RN), op=AL.mult),
                    waits=[t_tot, t_rn], inc="sV")
                B.op("vector", lambda e, MEAN=MEAN, AM=AM: e.tensor_tensor(
                    cell(AM), cell(MEAN), cell(EPS), op=AL.max), waits=[t_eps, t_m])
                B.op("vector", lambda e, AM=AM, THRc=THRc: e.tensor_tensor(
                    cell(THRc), cell(AM), cell(C07), op=AL.mult), waits=[t_c07])
                t_th = B.op("vector", lambda e, THRc=THRc, NTHRc=NTHRc: e.tensor_tensor(
                    cell(NTHRc), cell(CZERO), cell(THRc), op=AL.subtract), inc="sV")
                stats[wi] = [
                    bcast_b(cell(THRc), THRv, bank_free, 0, [t_th]),
                    bcast_b(cell(NTHRc), NTHRv, bank_free, 1, [t_th]),
                    bcast_b(cell(AM), Sv, bank_free, 2, [t_th]),
                ]
                tern_scale[wi] = B.op("vector", lambda e, Sv=Sv: e.tensor_single_scalar(
                    vcol(Sv), vcol(Sv), 0.5, op=AL.mult), waits=[stats[wi][2]], inc="sV")
                if wi == 0:
                    t_mz = B.op("gpsimd", lambda e: e.memset(vcol(ACC0), 0.0),
                                waits=[t_tot], inc="sG")
                    t_a = None
            yield
        tern_done[1] = [B.stream_last.get("tw0"), B.stream_last.get("tw1")]
        tern_final.extend([consumer_tok[0], consumer_tok[1], consumer_tok[2],
                           consumer_tok[3], q_free[0], q_free[1], t_q])

    XH, XL = 0, 65536
    XROW = [131072, 139264]
    xh = bb(XH, 32768)
    xl = bb(XL, 32768)
    bank_free = dict(bank_last)
    prev_row = [None, None]
    xrow_free = [None, None]
    for sci in range(16):
        xrow = bf(XROW[sci % 2], 2048)
        t_ld = B.dma(xrow, x_in[sci * 128:(sci + 1) * 128, :],
                     waits=[prev_row[sci % 2], t_bl if sci == 0 else None],
                     stream="xr%d" % (sci % 2))
        for hc in range(16):
            bk = (sci * 16 + hc) % 2
            t_tr = B.op("tensor", lambda e, xr=xrow, hc=hc, bk=bk: e.transpose(
                pb[bk][:, 0:128], xr[:, hc * 128:(hc + 1) * 128], idn),
                waits=[t_ld, bank_free[bk], t_g0], inc="sP")
            dst = hc * 2048 + sci * 128
            t_hi = B.op("scalar", lambda e, dst=dst, bk=bk: e.activation(
                xh[:, dst:dst + 128], pb[bk][:, 0:128], AF.Copy),
                waits=[t_tr], inc="sA")
            t_lo = B.op("vector", lambda e, dst=dst, bk=bk: e.tensor_tensor(
                xl[:, dst:dst + 128], pb[bk][:, 0:128], xh[:, dst:dst + 128],
                op=AL.subtract), waits=[t_tr, t_hi], inc="sV")
            bank_free[bk] = t_lo
            if hc == 15:
                prev_row[sci % 2] = t_tr
                xrow_free[sci % 2] = t_tr

    WQF = [131072, 139264]
    WQH = [147456, 155648]
    WQL = [151552, 159744]
    EH = [163840, 165888]
    EL = [164864, 166912]

    bank_free = {i: bank_free.get(i) for i in range(8)}
    tern_it = tern_units(bank_free)

    def pump(n):
        for _ in range(n):
            if next(tern_it, "done") == "done":
                break

    prev_pair = [None, None]
    prev_wqf = [xrow_free[0], xrow_free[1]]
    ev_free = [None, None]
    split_tok = [None, None]

    load_tok_w = [None, None]

    def qk_load(oc):
        wqf = bf(WQF[oc % 2], 2048)
        st_ = split_tok[oc % 2]
        load_tok_w[oc % 2] = B.dma(wqf, wqk[oc * 128:(oc + 1) * 128, :],
                                   waits=[prev_wqf[oc % 2],
                                          st_[1] if st_ else None],
                                   stream="wq%d" % (oc % 2))

    def qk_split(oc):
        wqf = bf(WQF[oc % 2], 2048)
        whi = bb(WQH[oc % 2], 2048)
        wlo = bb(WQL[oc % 2], 2048)
        t_hi = B.op("scalar", lambda e, w=wqf, h=whi: e.activation(h, w, AF.Copy),
                    waits=[load_tok_w[oc % 2], prev_pair[oc % 2]], inc="sA")
        t_lo = B.op("vector", lambda e, w=wqf, h=whi, l=wlo: e.tensor_tensor(
            l, w, h, op=AL.subtract),
            waits=[load_tok_w[oc % 2], t_hi, prev_pair[oc % 2]], inc="sV")
        split_tok[oc % 2] = (t_hi, t_lo)

    WOSF, WOSH, WOSL = 167936, 172032, 174080
    wos_state = {"tok": None, "wr_hi": None, "wr_lo": None}

    def wos_units():
        wosf = bf(WOSF, 1024)
        woshi = bb(WOSH, 1024)
        woslo = bb(WOSL, 1024)
        for hc in range(16):
            for oh in range(2):
                t_ld = B.dma(wosf, wot[hc * 128:(hc + 1) * 128, oh * 1024:(oh + 1) * 1024],
                             waits=[wos_state["tok"]] +
                             (tern_final if hc == 0 and oh == 0 else []),
                             stream="wsf")
                t_hi = B.op("scalar", lambda e: e.activation(woshi, wosf, AF.Copy),
                            waits=[t_ld, wos_state["wr_hi"]], inc="sA")
                t_lo = B.op("vector", lambda e: e.tensor_tensor(
                    woslo, wosf, woshi, op=AL.subtract),
                    waits=[t_ld, t_hi, wos_state["wr_lo"]], inc="sV")
                r0 = (hc * 2 + oh) * 128
                wos_state["wr_hi"] = B.dma(wos[r0:r0 + 128, 0:1024], woshi,
                                           waits=[t_hi], stream="ww0")
                wos_state["wr_lo"] = B.dma(wos[r0:r0 + 128, 1024:2048], woslo,
                                           waits=[t_lo], stream="ww1")
                wos_state["tok"] = t_lo
                yield

    wos_it = wos_units()
    VSTG = [180224, 182272]
    VEH = [184320, 185344, 186368, 187392]
    VEL = [188416, 189440, 190464, 191488]
    vstg_free = [None, None]
    veh_free = [None] * 4
    vel_free = [None] * 4
    vt4 = vt.rearrange("(kc hh p) c -> kc hh p c", hh=2, p=128)

    v_pending = []

    def v_emit(oc, ss, bk, ei, t_cp):
        hcv = oc - 32
        hh, pos = hcv // 8, (hcv % 8) * 128
        vstg = bf(VSTG[ei], 512)
        ei4 = (oc * 4 + ss) % 4
        tf = tern_final if oc == 32 else []
        vehw = bb(VEH[ei4], 512)
        velw = bb(VEL[ei4], 512)
        t_trs = []
        for j in range(4):
            t_trs.append(B.op("tensor", lambda e, bk=bk, j=j, vstg=vstg: e.matmul(
                pb[bk][:, j * 128:(j + 1) * 128], vstg[:, j * 128:(j + 1) * 128],
                idn, start=(j == 0), stop=(j == 3), is_transpose=True),
                waits=[t_cp] if j == 0 else (), inc="sP" if j == 3 else None))
        t_his = None
        for j in range(4):
            t_his = B.op("scalar", lambda e, bk=bk, j=j, vehw=vehw: e.activation(
                vehw[:, j * 128:(j + 1) * 128],
                pb[bk][:, j * 128:(j + 1) * 128], AF.Copy),
                waits=([t_trs[3], veh_free[ei4]] + tf) if j == 0 else (),
                inc="sA" if j == 3 else None)
        t_last_lo = None
        for j in range(4):
            t_last_lo = B.op("vector", lambda e, bk=bk, j=j, vehw=vehw, velw=velw: e.tensor_tensor(
                velw[:, j * 128:(j + 1) * 128],
                pb[bk][:, j * 128:(j + 1) * 128],
                vehw[:, j * 128:(j + 1) * 128], op=AL.subtract),
                waits=([t_trs[3], t_his, vel_free[ei4]] + tf) if j == 0 else (),
                inc="sV" if j == 3 else None)
        dst_hi = vt4[ss * 4:ss * 4 + 4, hh, :, pos:pos + 128].transpose([1, 0, 2])
        dst_lo = vt4[ss * 4:ss * 4 + 4, hh, :, 1024 + pos:1024 + pos + 128].transpose([1, 0, 2])
        veh_free[ei4] = B.dma(dst_hi, vehw, waits=[t_his], stream="vh%d" % ei4)
        vel_free[ei4] = B.dma(dst_lo, velw, waits=[t_last_lo], stream="vl%d" % ei4)
        vstg_free[ei] = t_trs[3]
        bank_free[bk] = t_last_lo

    qk_load(0)
    qk_load(1)
    qk_split(0)
    for oc in range(48):
        whi = bb(WQH[oc % 2], 2048)
        wlo = bb(WQL[oc % 2], 2048)
        t_hi, t_lo = split_tok[oc % 2]
        if oc + 1 < 48:
            qk_split(oc + 1)
        if oc + 2 < 48:
            qk_load(oc + 2)
        t_mm = None
        for ss in range(4):
            bk = ss + 4 * (oc % 2)
            n_mm = 0
            for hc in range(16):
                for (a, bmat) in ((whi, xh), (wlo, xh), (whi, xl)):
                    first = (n_mm == 0)
                    last = (n_mm == 47)
                    t_mm = B.op("tensor", lambda e, a=a, bmat=bmat, hc=hc, ss=ss, bk=bk, first=first, last=last: e.matmul(
                        pb[bk], a[:, hc * 128:(hc + 1) * 128],
                        bmat[:, hc * 2048 + ss * 512: hc * 2048 + ss * 512 + 512],
                        start=first, stop=last),
                        waits=[t_lo, t_hi, bank_free[bk]] if first else (),
                        inc="sP" if last else None)
                    n_mm += 1
            ei = (oc * 4 + ss) % 2
            if oc < 32:
                ehi = bb(EH[ei], 512)
                elo = bb(EL[ei], 512)
                t_ehi = B.op("scalar", lambda e, bk=bk, oc=oc, h=ehi: e.activation(
                    h, pb[bk], AF.Identity, bias=bqkv_sb[:, oc:oc + 1]),
                    waits=[t_mm, t_b1, ev_free[ei]], inc="sA")
                t_elo = B.op("vector", lambda e, bk=bk, oc=oc, h=ehi, l=elo: e.scalar_tensor_tensor(
                    l, pb[bk], bqkv_sb[:, oc:oc + 1], h, op0=AL.add, op1=AL.subtract),
                    waits=[t_mm, t_ehi, ev_free[ei]], inc="sV")
                bank_free[bk] = t_elo
                if oc < 16:
                    r0 = (oc * 4 + ss) * 128
                    B.dma(qf[r0:r0 + 128, 0:512], ehi, waits=[t_ehi],
                          stream="qw%d" % ei)
                    t_w2 = B.dma(qf[r0:r0 + 128, 512:1024], elo, waits=[t_elo],
                                 stream="qw%d" % ei)
                else:
                    hk = oc - 16
                    kh, pos = ss // 2, (ss % 2) * 512
                    r0 = (hk * 2 + kh) * 128
                    B.dma(kf[r0:r0 + 128, pos:pos + 512], ehi, waits=[t_ehi],
                          stream="qw%d" % ei)
                    t_w2 = B.dma(kf[r0:r0 + 128, 1024 + pos:1024 + pos + 512], elo,
                                 waits=[t_elo], stream="qw%d" % ei)
                ev_free[ei] = t_w2
            else:
                hcv = oc - 32
                ei = (oc * 4 + ss) % 2
                vstg = bf(VSTG[ei], 512)
                t_cp = B.op("scalar", lambda e, bk=bk, vstg=vstg: e.copy(
                    vstg, pb[bk]),
                    waits=[t_mm, vstg_free[ei]] +
                    (tern_final if oc == 32 and ss < 2 else []), inc="sA")
                pend = v_pending.pop(0) if v_pending else None
                v_pending.append((oc, ss, bk, ei, t_cp))
                if pend is not None:
                    v_emit(*pend)
            if ss == 3:
                prev_pair[oc % 2] = t_mm
                prev_wqf[oc % 2] = t_lo
            if oc < 32:
                pump(3)
            else:
                if ss == 1 or ss == 3:
                    next(wos_it, None)
    for pend in v_pending:
        v_emit(*pend)
    v_pending = []
    for _ in tern_it:
        pass
    for _ in wos_it:
        pass

    s3_end = [B.stream_last.get(s) for s in
              ("qw0", "qw1", "ww0", "ww1", "vh0", "vh1", "vh2", "vh3",
               "vl0", "vl1", "vl2", "vl3", "tw0", "tw1")] + \
        [B.last_tok["tensor"], B.last_tok["scalar"], B.last_tok["vector"]]

    QSL, SCO, PTH, PTL = 0, 32768, 65536, 81920
    KB = [98304, 102400]
    ESB, PN = 106496, 114688
    VB = [122880, 126976]
    CTH, CTL = 131072, 147456
    WOB = [163840, 167936]
    XRS = 98304
    YB, Y1B = 32768, 0
    DMP = 106496
    Y1FH, Y1FL = 65536, 81920
    QFCS = 131072
    RLS = [163840, 32768]
    T2S = [165888, 34816]
    QPB = [135168, 137216]
    OUTB = [65536, 73728]

    def hsq_off(hc):
        if hc < 16:
            return 98304 + hc * 2048
        if hc < 24:
            return 147456 + (hc - 16) * 2048
        return 167936 + (hc - 24) * 2048

    qsl = bb(QSL, 16384)
    sco = bf(SCO, 8192)
    pth = bb(PTH, 8192)
    ptl = bb(PTL, 8192)
    esb = bf(ESB, 2048)
    pn = bf(PN, 2048)
    cth = bb(CTH, 8192)
    ctl = bb(CTL, 8192)
    xrs = bf(XRS, 8192)
    yb = bf(YB, 8192)
    y1b = bf(Y1B, 8192)
    dmp = bf(DMP, 2048)
    y1fh = bb(Y1FH, 8192)
    y1fl = bb(Y1FL, 8192)
    qfcs = bb(QFCS, 8192)
    rls = [bf(RLS[0], 512), bf(RLS[1], 512)]
    t2s = [bf(T2S[0], 512), bf(T2S[1], 512)]

    grp_end = {k: None for k in ("y2v", "hap", "hlv", "out0", "out1", "pe",
                                 "kbfree")}
    grp_end["y2map"] = {}
    for g in range(4):
        s30 = s3_end if g == 0 else []

        def ge(*keys):
            return [grp_end[k] for k in keys]

        kb_slot_last = [None, None]
        t_kld0 = {}
        for hc in range(2):
            ki = hc % 2
            r0 = (hc * 2 + 0) * 128
            t_kld0[hc] = B.dma(bb(KB[ki], 2048), kf[r0:r0 + 128, :],
                               waits=ge("kbfree") + (s30 if hc == 0 else []),
                               stream="kbg%d" % ki, eng="gpsimd")
        t_qsl = [None] * 16
        hq_order = [h for h in range(16) if h % 4 < 2] + \
            [h for h in range(16) if h % 4 >= 2]
        for idx, hq in enumerate(hq_order):
            r0 = (hq * 4 + g) * 128
            ytok = grp_end["y2map"].get((hq // 4, (hq % 4) // 2, hq % 2))
            t_qsl[hq] = B.dma(qsl[:, hq * 1024:(hq + 1) * 1024], qf[r0:r0 + 128, :],
                              waits=[ytok],
                              stream="qg%d" % hq, eng="gpsimd")
        t_sco_ev = []
        for kh in range(2):
            t_kld = dict(t_kld0) if kh == 0 else {}

            def kb_load(hc, kh=kh):
                ki = (kh * 16 + hc) % 2
                r0 = (hc * 2 + kh) * 128
                t_kld[hc] = B.dma(bb(KB[ki], 2048), kf[r0:r0 + 128, :],
                                  waits=[kb_slot_last[ki]],
                                  stream="kb%d" % ki)

            if kh == 1:
                kb_load(0)
                kb_load(1)
            for hc in range(16):
                ki = (kh * 16 + hc) % 2
                kb = bb(KB[ki], 2048)
                t_hcmm = None
                for qt in range(4):
                    for ks in range(2):
                        bk = qt * 2 + ks
                        qoff = hc * 1024 + qt * 128
                        for ip, (a_off, b_off) in enumerate(((qoff, ks * 512),
                                                            (qoff + 512, ks * 512),
                                                            (qoff, 1024 + ks * 512))):
                            first = (hc == 0 and ip == 0)
                            end_of_acc = (hc == 15 and ip == 2)
                            end_of_kb = (qt == 3 and ks == 1 and ip == 2)
                            w = []
                            if first:
                                w = [bank_free[bk]]
                            if ip == 0 and qt == 0 and ks == 0:
                                w = list(w) + [t_kld[hc], t_qsl[hc]]
                            t_hcmm = B.op("tensor", lambda e, a_off=a_off, b_off=b_off, bk=bk, kb=kb, first=first, last=end_of_acc: e.matmul(
                                pb[bk], qsl[:, a_off:a_off + 128], kb[:, b_off:b_off + 512],
                                start=first, stop=last),
                                waits=w,
                                inc="sP" if (end_of_acc or end_of_kb) else None)
                            if end_of_acc:
                                acc_tok = t_hcmm
                                bank_free[bk] = None
                                t_sco_ev.append((bk, qt, ks, acc_tok))
                kb_slot_last[ki] = t_hcmm
                if hc + 2 < 16:
                    kb_load(hc + 2)
            ev_toks = []
            for (bk, qt, ks, acc_tok) in t_sco_ev[-8:]:
                dst = qt * 2048 + kh * 1024 + ks * 512
                eng = "scalar" if bk % 2 == 0 else "vector"
                exw = ge("hap", "hlv") if (kh == 0 and bk == 0) else []
                if eng == "scalar":
                    tok = B.op(eng, lambda e, bk=bk, dst=dst: e.copy(
                        sco[:, dst:dst + 512], pb[bk]),
                        waits=[acc_tok] + exw, inc="sA")
                else:
                    tok = B.op(eng, lambda e, bk=bk, dst=dst: e.tensor_copy(
                        sco[:, dst:dst + 512], pb[bk]),
                        waits=[acc_tok] + exw, inc="sV")
                bank_free[bk] = tok
                ev_toks.append(tok)
        sco_qt_toks = {qt: [] for qt in range(4)}
        for (bk, qt_, ks_, _a) in t_sco_ev:
            sco_qt_toks[qt_].append(bank_free[bk])
        t_sco_all = [t for t in (ev_toks + [bank_free[i] for i in range(8)])
                     if t is not None]
        t_qsl_lastread = kb_slot_last[1] or kb_slot_last[0]

        t_pt_done = []
        pn2 = [esb, pn]
        pn_read = [None, None]
        pn_norm = [None, None]
        t_tr = None
        for qt in range(4):
            pnb = pn2[qt % 2]
            t_nm = B.op("vector", lambda e, qt=qt: e.tensor_reduce(
                vcol(NMAX), sco[:, qt * 2048:(qt + 1) * 2048], axis=AX.X, op=AL.max,
                negate=True), waits=sco_qt_toks[qt] + [pn_norm[qt % 2]],
                inc="sV")
            t_eb = B.op("vector", lambda e: e.tensor_tensor(
                vcol(EBIAS), vcol(NMAX), vcol(SSCB), op=AL.mult),
                waits=[t_sscb], inc="sV")
            t_exp = B.op("scalar", lambda e, qt=qt, pnb=pnb: e.activation(
                pnb, sco[:, qt * 2048:(qt + 1) * 2048], AF.Exp,
                bias=vcol(EBIAS), scale=vcol(SSCB), accum_out=vcol(ESUM)),
                waits=[t_eb, pn_read[qt % 2]] + (ge("pe") if qt == 0 else []),
                inc="sA")
            t_rq = B.op("vector", lambda e: e.reciprocal(vcol(RQ), vcol(ESUM)),
                        waits=[t_exp], inc="sV")
            t_pn_c = []
            for c in range(4):
                t_pn_c.append(B.op("vector", lambda e, c=c, pnb=pnb: e.tensor_single_scalar(
                    pnb[:, c * 512:(c + 1) * 512], pnb[:, c * 512:(c + 1) * 512],
                    vcol(RQ), op=AL.mult),
                    waits=[t_rq] if c == 0 else (), inc="sV"))
            pn_norm[qt % 2] = t_pn_c[-1]
            for kc4 in range(4):
                bk = kc4 % 2
                t_tr = None
                for j in range(4):
                    kc = kc4 * 4 + j
                    t_tr = B.op("tensor", lambda e, kc=kc, j=j, bk=bk, pnb=pnb: e.matmul(
                        pb[bk][:, j * 128:(j + 1) * 128],
                        pnb[:, kc * 128:(kc + 1) * 128], idn,
                        start=(j == 0), stop=(j == 3), is_transpose=True),
                        waits=[t_pn_c[kc4], bank_free[bk]] if j == 0 else (),
                        inc="sP" if j == 3 else None)
                dsth = pth.rearrange("p (kc c) -> p kc c", c=512)[
                    :, kc4 * 4:kc4 * 4 + 4, qt * 128:qt * 128 + 128]
                dstl = ptl.rearrange("p (kc c) -> p kc c", c=512)[
                    :, kc4 * 4:kc4 * 4 + 4, qt * 128:qt * 128 + 128]
                t_hi = B.op("scalar", lambda e, bk=bk, dsth=dsth: e.activation(
                    dsth, pb[bk], AF.Copy),
                    waits=[t_tr] + (ge("out0", "out1") if qt == 0 and kc4 == 0 else []),
                    inc="sA")
                t_lo = B.op("vector", lambda e, bk=bk, dsth=dsth, dstl=dstl: e.tensor_tensor(
                    dstl, pb[bk], dsth, op=AL.subtract),
                    waits=[t_tr, t_hi], inc="sV")
                bank_free[bk] = t_lo
                t_pt_done.append(t_lo)
            pn_read[qt % 2] = t_tr
        t_sco_lastread = t_exp
        prev_pn_read = t_tr

        t_ct_done = []
        vb_slot_last = [None, None]
        for hh in range(2):
            t_vld = {}

            def vb_load(kc, hh=hh):
                vi = (hh * 16 + kc) % 2
                r0 = (kc * 2 + hh) * 128
                t_vld[kc] = B.dma(bb(VB[vi], 2048), vt[r0:r0 + 128, :],
                                  waits=[vb_slot_last[vi]] +
                                  (ge("pe") if hh == 0 and kc < 2 else []),
                                  stream="vb%d" % vi)

            vb_load(0)
            vb_load(1)
            t_kcmm = {}
            for kc in range(16):
                vi = (hh * 16 + kc) % 2
                vbuf = bb(VB[vi], 2048)
                t_last = None
                for h8 in range(8):
                    bk = h8
                    for ip in range(3):
                        a_off = h8 * 128 if ip != 1 else 1024 + h8 * 128
                        pmat = pth if ip != 2 else ptl
                        first = (kc == 0 and ip == 0)
                        end_of_acc = (kc == 15 and ip == 2)
                        end_of_vb = (h8 == 7 and ip == 2)
                        w = []
                        if first:
                            w = [t_pt_done[-1], bank_free[bk]]
                        if ip == 0 and h8 == 0:
                            w = list(w) + [t_vld[kc]]
                        t_last = B.op("tensor", lambda e, vbuf=vbuf, a_off=a_off, kc=kc, pmat=pmat, bk=bk, first=first, last=end_of_acc: e.matmul(
                            pb[bk], vbuf[:, a_off:a_off + 128],
                            pmat[:, kc * 512:kc * 512 + 512],
                            start=first, stop=last),
                            waits=w, inc="sP" if (end_of_acc or end_of_vb) else None)
                        if end_of_acc:
                            t_kcmm[bk] = t_last
                vb_slot_last[vi] = t_last
                if kc + 2 < 16:
                    vb_load(kc + 2)
            for h8 in range(8):
                hca = hh * 8 + h8
                dst = hca * 512
                t_hi = B.op("scalar", lambda e, bk=h8, dst=dst, hca=hca: e.activation(
                    cth[:, dst:dst + 512], pb[bk], AF.Identity,
                    bias=bqkv_sb[:, 32 + hca:33 + hca]),
                    waits=[t_kcmm[h8]], inc="sA")
                t_lo = B.op("vector", lambda e, bk=h8, dst=dst, hca=hca: e.scalar_tensor_tensor(
                    ctl[:, dst:dst + 512], pb[bk], bqkv_sb[:, 32 + hca:33 + hca],
                    cth[:, dst:dst + 512], op0=AL.add, op1=AL.subtract),
                    waits=[t_kcmm[h8], t_hi], inc="sV")
                bank_free[h8] = t_lo
                t_ct_done.append(t_lo)
        t_pt_lastread = vb_slot_last[1] or vb_slot_last[0]

        t_wld = {0: {}, 1: {}}
        wob_slot_last = [None, None]

        def wob_load(hc, oh):
            wi = (oh * 16 + hc) % 2
            r0 = (hc * 2 + oh) * 128
            t_wld[oh][hc] = B.dma(bb(WOB[wi], 2048), wos[r0:r0 + 128, :],
                                  waits=[wob_slot_last[wi]] +
                                  (ge("pe", "y2v") if oh == 0 and hc < 2 else []),
                                  stream="wo%d" % wi)

        wob_load(0, 0)
        wob_load(1, 0)
        t_xld = []
        for sci in range(4):
            t_xld.append(B.dma(xrs[:, sci * 2048:(sci + 1) * 2048],
                               x_in[g * 512 + sci * 128: g * 512 + sci * 128 + 128, :],
                               waits=[t_pt_lastread if sci == 0 else None,
                                      prev_pn_read if sci == 0 else None,
                                      t_ct_done[-1] if sci == 0 else None],
                               stream="xs"))
        t_y_done = []
        y_map = {}
        t_d_lastmm = None
        for oh in range(2):
            if oh == 1:
                wob_load(0, 1)
                wob_load(1, 1)
            for hc in range(16):
                wi = (oh * 16 + hc) % 2
                wbuf = bb(WOB[wi], 2048)
                t_last = None
                for sci in range(4):
                    for os2 in range(2):
                        bk = sci * 2 + os2
                        coff = hc * 512 + sci * 128
                        for ip in range(3):
                            amat = cth if ip != 1 else ctl
                            b_off = os2 * 512 if ip != 2 else 1024 + os2 * 512
                            first = (hc == 0 and ip == 0)
                            end_of_wb = (sci == 3 and os2 == 1 and ip == 2)
                            w = []
                            if first:
                                w = [t_ct_done[-1], bank_free[bk]]
                            if ip == 0 and sci == 0 and os2 == 0:
                                w = list(w) + [t_wld[oh][hc]]
                            t_last = B.op("tensor", lambda e, amat=amat, coff=coff, wbuf=wbuf, b_off=b_off, bk=bk, first=first: e.matmul(
                                pb[bk], amat[:, coff:coff + 128], wbuf[:, b_off:b_off + 512],
                                start=first, stop=False),
                                waits=w, inc="sP" if end_of_wb else None)
                wob_slot_last[wi] = t_last
                t_d_lastmm = t_last
                if hc + 2 < 16:
                    wob_load(hc + 2, oh)
            for bk in range(8):
                sci, os2 = bk // 2, bk % 2
                boff = oh * 1024 + os2 * 512
                B.op("tensor", lambda e, bk=bk, boff=boff: e.matmul(
                    pb[bk], ones_bf, bout_hi[:, boff:boff + 512],
                    start=False, stop=False), waits=[t_bh, t_g0] if g == 0 and oh == 0 and bk == 0 else ())
                t_bias = B.op("tensor", lambda e, bk=bk, boff=boff: e.matmul(
                    pb[bk], ones_bf, bout_lo[:, boff:boff + 512],
                    start=False, stop=True), waits=[t_bl] if g == 0 and oh == 0 and bk == 0 else (), inc="sP")
                dst = sci * 2048 + oh * 1024 + os2 * 512
                t_y = B.op("vector", lambda e, bk=bk, dst=dst: e.tensor_tensor(
                    yb[:, dst:dst + 512], pb[bk], xrs[:, dst:dst + 512], op=AL.add),
                    waits=[t_bias, t_xld[-1],
                           t_sco_lastread if oh == 0 and bk == 0 else None], inc="sV")
                bank_free[bk] = t_y
                y_map[(sci, oh, os2)] = t_y
                t_y_done.append(t_y)

        def layer_norm(src_view, dst_view, waits_first, sci, dump_extra=None, apply_chunks=1):
            base = 64 + sci * 9
            NS, NMc, SSQc, T1c, T2c = base, base + 1, base + 2, base + 3, base + 4
            Vc, SDc, RSc, NMRc = base + 5, base + 6, base + 7, base + 8
            t_ns = B.op("vector", lambda e: e.tensor_reduce(
                vcol(NS), src_view, axis=AX.X, op=AL.add, negate=True),
                waits=waits_first, inc="sV")
            B.op("vector", lambda e: e.tensor_single_scalar(
                vcol(NMc), vcol(NS), INV2048, op=AL.mult))
            t_sq_ = B.op("scalar", lambda e: e.activation(
                dmp, src_view, AF.Square, accum_out=vcol(SSQc)),
                waits=list(waits_first) + [dump_extra], inc="sA")
            B.op("vector", lambda e: e.tensor_single_scalar(
                vcol(T1c), vcol(SSQc), INV2048, op=AL.mult), waits=[t_sq_])
            B.op("vector", lambda e: e.tensor_tensor(vcol(T2c), vcol(NMc), vcol(NMc),
                                                     op=AL.mult))
            t_var = B.op("vector", lambda e: e.tensor_tensor(
                vcol(Vc), vcol(T1c), vcol(T2c), op=AL.subtract), inc="sV")
            t_sd = B.op("scalar", lambda e: e.activation(
                vcol(SDc), vcol(Vc), AF.Sqrt, bias=vcol(EPSB)),
                waits=[t_var, t_epsb], inc="sA")
            t_rs = B.op("vector", lambda e: e.reciprocal(vcol(RSc), vcol(SDc)),
                        waits=[t_sd], inc="sV")
            t_nmr = B.op("vector", lambda e: e.tensor_tensor(
                vcol(NMRc), vcol(NMc), vcol(RSc), op=AL.mult), waits=[t_rs], inc="sV")
            t_apc = []
            for c in range(apply_chunks):
                w_ = 2048 // apply_chunks
                t_apc.append(B.op("scalar", lambda e, c=c, w_=w_: e.activation(
                    dst_view[:, c * w_:(c + 1) * w_], src_view[:, c * w_:(c + 1) * w_],
                    AF.Identity, bias=vcol(NMRc), scale=vcol(RSc)),
                    waits=[t_nmr, t_rs] if c == 0 else (), inc="sA"))
            return t_apc, t_nmr

        t_y1_tr = []
        t_ln_prev = None
        ln1_vtok = [None] * 4
        for sci in range(4):
            y_toks = [y_map[(sci, oh_, os_)] for oh_ in range(2) for os_ in range(2)]
            t_apc, _ = layer_norm(
                yb[:, sci * 2048:(sci + 1) * 2048],
                y1b[:, sci * 2048:(sci + 1) * 2048],
                y_toks + [t_qsl_lastread if sci == 0 else None],
                sci, dump_extra=t_y_done[-1] if sci == 0 else None, apply_chunks=4)
            ln1_vtok[sci] = _
            t_ln_prev = t_apc[-1]
            for dc4 in range(4):
                bk = (sci * 4 + dc4) % 2
                t_tr = None
                for j in range(4):
                    src_off = sci * 2048 + (dc4 * 4 + j) * 128
                    t_tr = B.op("tensor", lambda e, src_off=src_off, j=j, bk=bk: e.matmul(
                        pb[bk][:, j * 128:(j + 1) * 128],
                        y1b[:, src_off:src_off + 128], idn,
                        start=(j == 0), stop=(j == 3), is_transpose=True),
                        waits=[t_apc[dc4], bank_free[bk]] if j == 0 else (),
                        inc="sP" if j == 3 else None)
                dsth = y1fh.rearrange("p (dc c) -> p dc c", c=512)[
                    :, dc4 * 4:dc4 * 4 + 4, sci * 128:sci * 128 + 128]
                dstl = y1fl.rearrange("p (dc c) -> p dc c", c=512)[
                    :, dc4 * 4:dc4 * 4 + 4, sci * 128:sci * 128 + 128]
                t_hi = B.op("scalar", lambda e, bk=bk, dsth=dsth: e.activation(
                    dsth, pb[bk], AF.Copy), waits=[t_tr], inc="sA")
                t_lo = B.op("vector", lambda e, bk=bk, dsth=dsth, dstl=dstl: e.tensor_tensor(
                    dstl, pb[bk], dsth, op=AL.subtract),
                    waits=[t_tr, t_hi], inc="sV")
                bank_free[bk] = t_lo
                t_y1_tr.append(t_lo)

        t_hsq_done = []
        t_qf_free = None
        t_rl_free = [None, None]
        t_t2_free = [None, None]
        for hcb in range(10):
            t_qld = []
            for dc in range(16):
                t_qld.append(B.dma(qfcs[:, dc * 512:(dc + 1) * 512],
                             qfc[dc * 128:(dc + 1) * 128, hcb * 512:(hcb + 1) * 512],
                             waits=[t_qf_free if dc == 0 else None,
                                    t_d_lastmm if hcb == 0 and dc == 0 else None] +
                             (tern_done[0] if g == 0 and hcb == 0 and dc == 0 else []),
                             stream="ts%d" % dc))
            t_mmf = {}
            for h4 in range(4):
                bk = h4 + 4 * (hcb % 2)
                n_mm = 0
                for dc in range(16):
                    for amat in (y1fh, y1fl):
                        first = (n_mm == 0)
                        last = (n_mm == 31)
                        w = []
                        if first:
                            w = [t_y1_tr[-1], bank_free[bk]]
                        if h4 == 0 and amat is y1fh:
                            w = list(w) + [t_qld[dc]]
                        t_mmf[bk] = B.op("tensor", lambda e, dc=dc, h4=h4, amat=amat, bk=bk, first=first, last=last: e.matmul(
                            pb[bk], qfcs[:, dc * 512 + h4 * 128: dc * 512 + h4 * 128 + 128],
                            amat[:, dc * 512:(dc + 1) * 512],
                            start=first, stop=last),
                            waits=w, inc="sP" if last else None)
                        n_mm += 1
                hca = hcb * 4 + h4
                hoff = hsq_off(hca)
                pi = hca % 2
                rl = rls[pi]
                t2 = t2s[pi]
                t_rl = B.op("scalar", lambda e, bk=bk, rl=rl: e.activation(
                    rl, pb[bk], AF.Relu, scale=vcol(SFCB)),
                    waits=[t_mmf[bk], tern_scale[0], t_rl_free[pi]] +
                    ([ln1_vtok[0]] if hca < 2 else []), inc="sA")
                t_t2 = B.op("vector", lambda e, rl=rl, t2=t2: e.tensor_tensor(
                    t2, rl, rl, op=AL.mult),
                    waits=[t_rl, t_t2_free[pi]], inc="sV")
                t_hi = B.op("scalar", lambda e, hoff=hoff, t2=t2: e.activation(
                    bb(hoff, 512), t2, AF.Copy), waits=[t_t2], inc="sA")
                t_lo = B.op("vector", lambda e, hoff=hoff, t2=t2: e.tensor_tensor(
                    bb(hoff + 1024, 512), t2, bb(hoff, 512), op=AL.subtract),
                    waits=[t_hi], inc="sV")
                bank_free[bk] = t_rl
                t_rl_free[pi] = t_t2
                t_t2_free[pi] = t_lo
                t_hsq_done.append(t_lo)
            t_qf_free = t_mmf[3 + 4 * (hcb % 2)]

        t_y2_done = []
        g_y2map = {}
        g_kbfree_new = [None]
        qp_slot_last = [t_qf_free, t_qf_free]
        t_mmp2 = {}
        for dh in range(2):
            t_qpld = {}

            def qp_load(hc, dh=dh):
                qi = (dh * FC + hc) % 2
                t_qpld[hc] = B.dma(bb(QPB[qi], 1024),
                                   qpr[hc * 128:(hc + 1) * 128, dh * 1024:(dh + 1) * 1024],
                                   waits=[qp_slot_last[qi]] +
                                   (tern_done[1] if g == 0 and dh == 0 and hc < 2 else []),
                                   stream="qp%d" % qi)

            qp_load(0)
            qp_load(1)
            for hc in range(FC):
                qi = (dh * FC + hc) % 2
                qpb = bb(QPB[qi], 1024)
                hoff = hsq_off(hc)
                t_last = None
                for sci in range(4):
                    for ds in range(2):
                        bk = sci * 2 + ds
                        for ip in range(2):
                            a_ap = bb(hoff + (0 if ip == 0 else 1024), 512)
                            first = (hc == 0 and ip == 0)
                            end_of_acc = (hc == FC - 1 and ip == 1)
                            end_of_qp = (sci == 3 and ds == 1 and ip == 1)
                            w = []
                            if first:
                                w = [t_hsq_done[-1], bank_free[bk]]
                            if ip == 0 and sci == 0 and ds == 0:
                                w = list(w) + [t_qpld[hc]]
                            t_last = B.op("tensor", lambda e, a_ap=a_ap, sci=sci, qpb=qpb, ds=ds, bk=bk, first=first, last=end_of_acc: e.matmul(
                                pb[bk], a_ap[:, sci * 128:sci * 128 + 128],
                                qpb[:, ds * 512:ds * 512 + 512],
                                start=first, stop=last),
                                waits=w, inc="sP" if (end_of_acc or end_of_qp) else None)
                            if end_of_acc:
                                t_mmp2[bk] = t_last
                qp_slot_last[qi] = t_last
                if dh == 1 and hc == 3:
                    g_kbfree_new[0] = t_last
                if hc + 2 < FC:
                    qp_load(hc + 2)
            for bk in range(8):
                sci, ds = bk // 2, bk % 2
                dst = sci * 2048 + dh * 1024 + ds * 512
                t_y2 = B.op("vector", lambda e, bk=bk, dst=dst: e.scalar_tensor_tensor(
                    yb[:, dst:dst + 512], pb[bk], vcol(SPRB), y1b[:, dst:dst + 512],
                    op0=AL.mult, op1=AL.add),
                    waits=[t_mmp2[bk], tern_scale[1]], inc="sV")
                bank_free[bk] = t_y2
                g_y2map[(sci, dh, ds)] = t_y2
                t_y2_done.append(t_y2)

        t_ln_prev = None
        t_ln_v = None
        outb_free = [None, None]
        t_out_wr = None
        for sci in range(4):
            ob = bf(OUTB[sci % 2], 2048)
            y2_toks = [g_y2map[(sci, dh_, ds_)] for dh_ in range(2) for ds_ in range(2)]
            t_apc, t_ln_v = layer_norm(
                yb[:, sci * 2048:(sci + 1) * 2048], ob,
                y2_toks + [outb_free[sci % 2]],
                sci, apply_chunks=2)
            t_ln_prev = t_apc[-1]
            t_out_wr = None
            for c in range(2):
                t_out_wr = B.dma(
                    out_hbm[g * 512 + sci * 128: g * 512 + sci * 128 + 128,
                            c * 1024:(c + 1) * 1024],
                    ob[:, c * 1024:(c + 1) * 1024],
                    waits=[t_apc[c], t_out_wr], stream="ou%d" % (sci % 2))
            outb_free[sci % 2] = t_out_wr
        grp_end = {"y2v": t_y2_done[-1], "hap": t_ln_prev, "hlv": t_ln_v,
                   "out0": outb_free[0], "out1": outb_free[1],
                   "pe": t_mmp2[7], "y2map": g_y2map, "kbfree": g_kbfree_new[0]}

    assert nc.sbuf_base <= my_base, (nc.sbuf_base, my_base)
    B.op("sync", lambda e: e.nop(),
         waits=[("D_ou0", B.plan["D_ou0"], "x"), ("D_ou1", B.plan["D_ou1"], "x")])
    B.finalize()
    return nc


_NC_CACHE = []


def kernel(L0_attn_hidden_states, L0_attn_w_qkv, L0_attn_b_qkv, L0_attn_w_out,
           L0_attn_b_out, L0_ffn_w_fc, L0_ffn_w_proj):
    x = np.asarray(L0_attn_hidden_states, dtype=np.float32)
    wqkv = np.asarray(L0_attn_w_qkv, dtype=np.float32)
    bqkv = np.asarray(L0_attn_b_qkv, dtype=np.float32)
    wout = np.asarray(L0_attn_w_out, dtype=np.float32)
    bout = np.asarray(L0_attn_b_out, dtype=np.float32)
    wfc = np.asarray(L0_ffn_w_fc, dtype=np.float32)
    wproj = np.asarray(L0_ffn_w_proj, dtype=np.float32)

    w1 = wqkv.reshape(48, 128, 16, 128)
    wqk_t = np.ascontiguousarray(w1.transpose(0, 3, 2, 1)).reshape(48 * 128, 2048)
    bqkv_t = np.ascontiguousarray(bqkv.reshape(48, 128).T)
    bout_r = np.ascontiguousarray(bout.reshape(1, 2048))
    wot_t = np.ascontiguousarray(wout.T).reshape(16 * 128, 2048)
    wfct_t = np.ascontiguousarray(wfc.T).reshape(16 * 128, 5120)
    wprt_t = np.ascontiguousarray(wproj.T).reshape(40 * 128, 2048)

    if not _NC_CACHE:
        _NC_CACHE.append(build_core())
    nc = _NC_CACHE[0]
    shared = dict(wqk=wqk_t, bqkv=bqkv_t, bout=bout_r,
                  wot=wot_t, wfct=wfct_t, wprt=wprt_t)
    in_maps = [dict(shared, x=np.ascontiguousarray(x[i])) for i in range(8)]
    res = run_bass_kernel_spmd(nc, in_maps, core_ids=list(range(8)))
    return np.stack([np.asarray(res.results[i]["out"]) for i in range(8)], axis=0)


# version 29
# speedup vs baseline: 1.0323x; 1.0151x over previous
import numpy as np
import concourse.bass as bass
import concourse.mybir as mybir
from concourse.bass_utils import run_bass_kernel_spmd

F32 = mybir.dt.float32
BF16 = mybir.dt.bfloat16
AL = mybir.AluOpType
AF = mybir.ActivationFunctionType
AX = mybir.AxisListType

ENGINES = ("sync", "gpsimd", "scalar", "vector", "tensor")

S = 2048
H = 2048
FC = 40
INV2048 = 1.0 / 2048.0


class Builder:
    def __init__(self, nc):
        self.nc = nc
        self.ops = {e: [] for e in ENGINES}
        self.plan = {}
        self.sem_names = []
        self.last_tok = {e: None for e in ENGINES}

    def sem(self, name):
        if name not in self.plan:
            self.plan[name] = 0
            self.sem_names.append(name)

    ENGSEM = {"vector": "sV", "scalar": "sA", "gpsimd": "sG"}

    def op(self, eng, fn, waits=(), inc=None, n=1):
        self_wait = None
        if eng in self.ENGSEM:
            sname = self.ENGSEM[eng]
            self.sem(sname)
            prev = self.plan[sname]
            if prev > 0:
                self_wait = (sname, prev, "_self")
            if inc is None:
                inc = sname
        tok = None
        if inc is not None:
            self.sem(inc)
            self.plan[inc] += n
            tok = (inc, self.plan[inc], eng)
            self.last_tok[eng] = tok
        ws = []
        for w in list(waits) + [self_wait]:
            if w is None:
                continue
            sname, v, peng = w
            ws.append((sname, v))
        self.ops[eng].append((fn, ws, (inc, n) if inc is not None else None))
        return tok

    def dma(self, out, in_, waits=(), stream="dL", eng="sync"):
        tok = self.op(eng, lambda e: e.dma_start(out=out, in_=in_),
                      waits=waits, inc="D_" + stream, n=16)
        if not hasattr(self, "stream_last"):
            self.stream_last = {}
        self.stream_last[stream] = tok
        return tok

    def finalize(self):
        nc = self.nc
        import contextlib
        stack = contextlib.ExitStack()
        handles = {}
        for name in self.sem_names:
            handles[name] = stack.enter_context(nc.semaphore(name))
        with stack:
            with nc.Block() as block:
                def make(eng_name):
                    def run(eng):
                        satisfied = {}
                        for fn, ws, inc in self.ops[eng_name]:
                            for sname, v in ws:
                                if satisfied.get(sname, -1) >= v:
                                    continue
                                eng.wait_ge(handles[sname], v)
                                satisfied[sname] = v
                            ins = fn(eng)
                            if inc is not None:
                                ins.then_inc(handles[inc[0]], inc[1])
                    return run
                for eng_name in ENGINES:
                    getattr(block, eng_name)(make(eng_name))


def build_core():
    nc = bass.Bass("TRN2", target_bir_lowering=False)

    x_in = nc.dram_tensor("x", [S, H], F32, kind="ExternalInput").ap()
    wqk = nc.dram_tensor("wqk", [48 * 128, 2048], F32, kind="ExternalInput").ap()
    bqkv_in = nc.dram_tensor("bqkv", [128, 48], F32, kind="ExternalInput").ap()
    bout_in = nc.dram_tensor("bout", [1, 2048], F32, kind="ExternalInput").ap()
    wot = nc.dram_tensor("wot", [16 * 128, 2048], F32, kind="ExternalInput").ap()
    wfct = nc.dram_tensor("wfct", [16 * 128, 5120], F32, kind="ExternalInput").ap()
    wprt = nc.dram_tensor("wprt", [40 * 128, 2048], F32, kind="ExternalInput").ap()
    out_hbm = nc.dram_tensor("out", [S, H], F32, kind="ExternalOutput").ap()

    qf = nc.dram_tensor("qf", [16 * 4 * 128, 1024], BF16).ap()
    kf = nc.dram_tensor("kf", [16 * 2 * 128, 2048], BF16).ap()
    vt = nc.dram_tensor("vt", [16 * 2 * 128, 2048], BF16).ap()
    wos = nc.dram_tensor("wos", [16 * 2 * 128, 2048], BF16).ap()
    qfc = nc.dram_tensor("qfc", [16 * 128, 5120], BF16).ap()
    qpr = nc.dram_tensor("qpr", [40 * 128, 2048], BF16).ap()

    AR_B = 200704
    my_base = (nc.sbuf_top - AR_B) & ~31
    _bufs = {}

    def _at(off, n, dt, tag):
        key = (off, n, tag)
        if key not in _bufs:
            _bufs[key] = nc.alloc_sbuf_tensor_at(
                "%s_%d_%d" % (tag, off, n), [128, n], dt,
                offset=my_base + off).ap()
        return _bufs[key]

    def bb(off, n):
        assert off % 2 == 0 and off + 2 * n <= AR_B, (off, n)
        return _at(off, n, BF16, "ab")

    def bf(off, n):
        assert off % 4 == 0 and off + 4 * n <= AR_B, (off, n)
        return _at(off, n, F32, "af")

    idn = nc.alloc_sbuf_tensor("idn", [128, 128], F32).ap()
    ones_row = nc.alloc_sbuf_tensor("ones_row", [1, 128], F32).ap()
    ones_bf = nc.alloc_sbuf_tensor("ones_bf", [1, 128], BF16).ap()
    bout_hi = nc.alloc_sbuf_tensor("bout_hi", [1, 2048], BF16).ap()
    bout_lo = nc.alloc_sbuf_tensor("bout_lo", [1, 2048], BF16).ap()
    bqkv_sb = nc.alloc_sbuf_tensor("bqkv_sb", [128, 48], F32).ap()
    st = nc.alloc_sbuf_tensor("st", [128, 128], F32).ap()

    pb = [nc.alloc_psum_tensor("pb%d" % i, [128, 512], F32).ap() for i in range(8)]

    B = Builder(nc)
    for s_ in ("dL", "dS", "dO", "sP", "sA", "sV", "sG"):
        B.sem(s_)

    ONESC, SSCB, EPSB = 0, 1, 2
    THRF, NTHRF, THRP, NTHRP, SFCB, SPRB = 3, 4, 5, 6, 7, 8
    NMAX, EBIAS, ESUM, RQ = 9, 10, 11, 12
    NSUM, NM, SSQ, T1, T2M, VAR, SD, RSTD, NMR = 13, 14, 15, 16, 17, 18, 19, 20, 21
    TMP0, ACC0 = 22, 23
    C2048, CN, C7, C10, C1E5, CZERO = 40, 41, 42, 43, 44, 45
    SQ2048, SSC, C07, EPS = 46, 47, 48, 49
    TOTF, MEANF, AMF, THRFs, NTHRFs = 50, 51, 52, 53, 54
    TOTP, MEANP, AMP, THRPs, NTHRPs = 55, 56, 57, 58, 59
    TMP1, RN = 60, 61

    def cell(c):
        return st[0:1, c:c + 1]

    def vcol(c):
        return st[:, c:c + 1]

    t_mi = B.op("gpsimd", lambda e: e.memset(idn, 0.0), inc="sG")
    B.op("gpsimd", lambda e: e.affine_select(
        out=idn, in_=idn, compare_op=AL.not_equal, fill=1.0, base=0,
        pattern=[[-1, 128]], channel_multiplier=1), waits=[t_mi], inc="sG")
    B.op("gpsimd", lambda e: e.memset(ones_row, 1.0), inc="sG")
    B.op("gpsimd", lambda e: e.memset(vcol(ONESC), 1.0), inc="sG")
    for c, v in ((C2048, 2048.0), (CN, 10485760.0), (C7, 7.0), (C10, 10.0),
                 (C1E5, 100000.0), (CZERO, 0.0)):
        B.op("gpsimd", lambda e, c=c, v=v: e.memset(cell(c), v), inc="sG")
    t_g0 = ("sG", B.plan["sG"], "gpsimd")

    t_b1 = B.dma(bqkv_sb, bqkv_in[:, :], stream="b0")
    bout_stage = bf(131072, 2048)[0:1, :]
    t_b2 = B.dma(bout_stage, bout_in[:, :], stream="b1")
    B.op("gpsimd", lambda e: e.memset(ones_bf, 1.0), inc="sG")
    t_bh = B.op("scalar", lambda e: e.activation(bout_hi, bout_stage, AF.Copy),
                waits=[t_b2], inc="sA")
    t_bl = B.op("vector", lambda e: e.tensor_tensor(bout_lo, bout_stage, bout_hi,
                                                    op=AL.subtract),
                waits=[t_b2, t_bh], inc="sV")

    t_sq0 = B.op("scalar", lambda e: e.activation(cell(SQ2048), cell(C2048), AF.Sqrt),
                 waits=[t_g0], inc="sA")
    t_ssc = B.op("vector", lambda e: e.reciprocal(cell(SSC), cell(SQ2048)),
                 waits=[t_sq0, t_g0], inc="sV")
    t_r10 = B.op("vector", lambda e: e.reciprocal(cell(TMP1), cell(C10)), waits=[t_g0],
                 inc="sV")
    t_c07 = B.op("vector", lambda e: e.tensor_tensor(cell(C07), cell(C7), cell(TMP1),
                                                     op=AL.mult), waits=[t_r10], inc="sV")
    t_rn = B.op("vector", lambda e: e.reciprocal(cell(RN), cell(CN)), inc="sV")
    t_eps = B.op("vector", lambda e: e.reciprocal(cell(EPS), cell(C1E5)), inc="sV")

    bank_last = {i: None for i in range(8)}

    def bcast(src_cell, dst_col, bank, waits):
        t_mm = B.op("tensor", lambda e, s=src_cell, b=bank: e.matmul(
            pb[b][:, 0:1], ones_row, s, start=True, stop=True),
            waits=list(waits) + [bank_last[bank], t_g0], inc="sP")
        t_cp = B.op("vector", lambda e, d=dst_col, b=bank: e.tensor_copy(
            vcol(d), pb[b][:, 0:1]), waits=[t_mm], inc="sV")
        bank_last[bank] = t_cp
        return t_cp

    t_sscb = bcast(cell(SSC), SSCB, 0, [t_ssc])
    t_epsb = bcast(cell(EPS), EPSB, 1, [t_eps])

    TZS = 167936
    tz_wf = [bf(TZS + k * 4096, 1024) for k in range(4)]
    tz_ta = bf(TZS + 16384, 1024)
    tz_tb = bf(TZS + 20480, 1024)
    tz_q = [bb(TZS + 24576, 1024), bb(TZS + 26624, 1024)]

    tern_done = {}
    tern_scale = {}
    tern_final = []

    def bcast_b(src_cell, dst_col, bank_free, bk, waits):
        t_mm = B.op("tensor", lambda e, s=src_cell, b=bk: e.matmul(
            pb[b][:, 0:1], ones_row, s, start=True, stop=True),
            waits=list(waits) + [bank_free[bk]], inc="sP")
        t_cp = B.op("vector", lambda e, d=dst_col, b=bk: e.tensor_copy(
            vcol(d), pb[b][:, 0:1]), waits=[t_mm], inc="sV")
        bank_free[bk] = t_cp
        return t_cp

    def tern_units(bank_free):
        P1F = [(wfct, i, t0) for i in range(16) for t0 in range(0, 5120, 1024)]
        P1P = [(wprt, i, t0) for i in range(40) for t0 in range(0, 2048, 1024)]
        sched = P1F + P1P + P1F + P1P
        n1 = len(P1F) + len(P1P)
        consumer_tok = [None] * 4
        load_tok = [None] * len(sched)
        n_loaded = [0]

        def issue_load(k):
            sl = k % 4
            wsrc, i, t0 = sched[k]
            load_tok[k] = B.dma(tz_wf[sl], wsrc[i * 128:(i + 1) * 128, t0:t0 + 1024],
                                waits=[consumer_tok[sl]], stream="tzs%d" % sl,
                                eng="gpsimd")
            n_loaded[0] += 1

        for k in range(min(4, len(sched))):
            issue_load(k)

        t_mz = B.op("gpsimd", lambda e: e.memset(vcol(ACC0), 0.0),
                    waits=[t_g0], inc="sG")
        stats = {}
        t_a = None
        t_q = None
        q_free = [None, None]
        for k in range(len(sched)):
            sl = k % 4
            buf = tz_wf[sl]
            if k < n1:
                t_ab = B.op("scalar", lambda e, b=buf: e.activation(
                    tz_ta, b, AF.Abs, accum_out=vcol(TMP0)),
                    waits=[load_tok[k], t_mz, t_a], inc="sA")
                t_a = B.op("vector", lambda e: e.tensor_tensor(
                    vcol(ACC0), vcol(ACC0), vcol(TMP0), op=AL.add),
                    waits=[t_ab], inc="sV")
                consumer_tok[sl] = t_ab
            else:
                kk = k - n1
                wi = 0 if kk < len(P1F) else 1
                wsrc, i, t0 = sched[k]
                dst = qfc if wi == 0 else qpr
                THRv, NTHRv = (THRF, NTHRF) if wi == 0 else (THRP, NTHRP)
                qsl_ = (k % 2)
                qb = tz_q[qsl_]
                t_s1 = B.op("scalar", lambda e, b=buf, tc=NTHRv: e.activation(
                    tz_ta, b, AF.Sign, bias=vcol(tc)),
                    waits=[load_tok[k], t_q] + (stats[wi] if (kk == 0 or kk == len(P1F)) else []),
                    inc="sA")
                t_s2 = B.op("scalar", lambda e, b=buf, tc=THRv: e.activation(
                    tz_tb, b, AF.Sign, bias=vcol(tc)), inc="sA")
                t_q = B.op("vector", lambda e, qb=qb: e.tensor_tensor(
                    qb, tz_ta, tz_tb, op=AL.add),
                    waits=[q_free[qsl_], t_s2], inc="sV")
                q_free[qsl_] = B.dma(dst[i * 128:(i + 1) * 128, t0:t0 + 1024], qb,
                                     waits=[t_q], stream="tw%d" % qsl_, eng="gpsimd")
                consumer_tok[sl] = t_s2
                if kk == len(P1F) - 1:
                    tern_done[0] = [B.stream_last.get("tw0"), B.stream_last.get("tw1")]
            if n_loaded[0] < len(sched):
                issue_load(n_loaded[0])
            if k == len(P1F) - 1 or k == n1 - 1:
                wi = 0 if k == len(P1F) - 1 else 1
                TOT, MEAN, AM, THRc, NTHRc = ((TOTF, MEANF, AMF, THRFs, NTHRFs) if wi == 0
                                              else (TOTP, MEANP, AMP, THRPs, NTHRPs))
                THRv, NTHRv, Sv = ((THRF, NTHRF, SFCB) if wi == 0 else (THRP, NTHRP, SPRB))
                t_tot = B.op("gpsimd", lambda e, TOT=TOT: e.tensor_reduce(
                    cell(TOT), vcol(ACC0), axis=AX.C, op=AL.add),
                    waits=[t_a], inc="sG")
                t_m = B.op("vector", lambda e, TOT=TOT, MEAN=MEAN: e.tensor_tensor(
                    cell(MEAN), cell(TOT), cell(RN), op=AL.mult),
                    waits=[t_tot, t_rn], inc="sV")
                B.op("vector", lambda e, MEAN=MEAN, AM=AM: e.tensor_tensor(
                    cell(AM), cell(MEAN), cell(EPS), op=AL.max), waits=[t_eps, t_m])
                B.op("vector", lambda e, AM=AM, THRc=THRc: e.tensor_tensor(
                    cell(THRc), cell(AM), cell(C07), op=AL.mult), waits=[t_c07])
                t_th = B.op("vector", lambda e, THRc=THRc, NTHRc=NTHRc: e.tensor_tensor(
                    cell(NTHRc), cell(CZERO), cell(THRc), op=AL.subtract), inc="sV")
                stats[wi] = [
                    bcast_b(cell(THRc), THRv, bank_free, 0, [t_th]),
                    bcast_b(cell(NTHRc), NTHRv, bank_free, 1, [t_th]),
                    bcast_b(cell(AM), Sv, bank_free, 2, [t_th]),
                ]
                tern_scale[wi] = B.op("vector", lambda e, Sv=Sv: e.tensor_single_scalar(
                    vcol(Sv), vcol(Sv), 0.5, op=AL.mult), waits=[stats[wi][2]], inc="sV")
                if wi == 0:
                    t_mz = B.op("gpsimd", lambda e: e.memset(vcol(ACC0), 0.0),
                                waits=[t_tot], inc="sG")
                    t_a = None
            yield
        tern_done[1] = [B.stream_last.get("tw0"), B.stream_last.get("tw1")]
        tern_final.extend([consumer_tok[0], consumer_tok[1], consumer_tok[2],
                           consumer_tok[3], q_free[0], q_free[1], t_q])

    XH, XL = 0, 65536
    XROW = [131072, 139264]
    xh = bb(XH, 32768)
    xl = bb(XL, 32768)
    bank_free = dict(bank_last)
    prev_row = [None, None]
    xrow_free = [None, None]
    for sci in range(16):
        xrow = bf(XROW[sci % 2], 2048)
        t_ld = B.dma(xrow, x_in[sci * 128:(sci + 1) * 128, :],
                     waits=[prev_row[sci % 2], t_bl if sci == 0 else None],
                     stream="xr%d" % (sci % 2))
        for hc in range(16):
            bk = (sci * 16 + hc) % 2
            t_tr = B.op("tensor", lambda e, xr=xrow, hc=hc, bk=bk: e.transpose(
                pb[bk][:, 0:128], xr[:, hc * 128:(hc + 1) * 128], idn),
                waits=[t_ld, bank_free[bk], t_g0], inc="sP")
            dst = hc * 2048 + sci * 128
            t_hi = B.op("scalar", lambda e, dst=dst, bk=bk: e.activation(
                xh[:, dst:dst + 128], pb[bk][:, 0:128], AF.Copy),
                waits=[t_tr], inc="sA")
            t_lo = B.op("vector", lambda e, dst=dst, bk=bk: e.tensor_tensor(
                xl[:, dst:dst + 128], pb[bk][:, 0:128], xh[:, dst:dst + 128],
                op=AL.subtract), waits=[t_tr, t_hi], inc="sV")
            bank_free[bk] = t_lo
            if hc == 15:
                prev_row[sci % 2] = t_tr
                xrow_free[sci % 2] = t_tr

    WQF = [131072, 139264]
    WQH = [147456, 155648]
    WQL = [151552, 159744]
    EH = [163840, 165888]
    EL = [164864, 166912]

    bank_free = {i: bank_free.get(i) for i in range(8)}
    tern_it = tern_units(bank_free)

    def pump(n):
        for _ in range(n):
            if next(tern_it, "done") == "done":
                break

    prev_pair = [None, None]
    prev_wqf = [xrow_free[0], xrow_free[1]]
    ev_free = [None, None]
    split_tok = [None, None]

    load_tok_w = [None, None]

    def qk_load(oc):
        wqf = bf(WQF[oc % 2], 2048)
        st_ = split_tok[oc % 2]
        load_tok_w[oc % 2] = B.dma(wqf, wqk[oc * 128:(oc + 1) * 128, :],
                                   waits=[prev_wqf[oc % 2],
                                          st_[1] if st_ else None],
                                   stream="wq%d" % (oc % 2))

    def qk_split(oc):
        wqf = bf(WQF[oc % 2], 2048)
        whi = bb(WQH[oc % 2], 2048)
        wlo = bb(WQL[oc % 2], 2048)
        t_hi = B.op("scalar", lambda e, w=wqf, h=whi: e.activation(h, w, AF.Copy),
                    waits=[load_tok_w[oc % 2], prev_pair[oc % 2]], inc="sA")
        t_lo = B.op("vector", lambda e, w=wqf, h=whi, l=wlo: e.tensor_tensor(
            l, w, h, op=AL.subtract),
            waits=[load_tok_w[oc % 2], t_hi, prev_pair[oc % 2]], inc="sV")
        split_tok[oc % 2] = (t_hi, t_lo)

    WOSF, WOSH, WOSL = 167936, 172032, 174080
    wos_state = {"tok": None, "wr_hi": None, "wr_lo": None}

    def wos_units():
        wosf = bf(WOSF, 1024)
        woshi = bb(WOSH, 1024)
        woslo = bb(WOSL, 1024)
        for hc in range(16):
            for oh in range(2):
                t_ld = B.dma(wosf, wot[hc * 128:(hc + 1) * 128, oh * 1024:(oh + 1) * 1024],
                             waits=[wos_state["tok"]] +
                             (tern_final if hc == 0 and oh == 0 else []),
                             stream="wsf")
                t_hi = B.op("scalar", lambda e: e.activation(woshi, wosf, AF.Copy),
                            waits=[t_ld, wos_state["wr_hi"]], inc="sA")
                t_lo = B.op("vector", lambda e: e.tensor_tensor(
                    woslo, wosf, woshi, op=AL.subtract),
                    waits=[t_ld, t_hi, wos_state["wr_lo"]], inc="sV")
                r0 = (hc * 2 + oh) * 128
                wos_state["wr_hi"] = B.dma(wos[r0:r0 + 128, 0:1024], woshi,
                                           waits=[t_hi], stream="ww0")
                wos_state["wr_lo"] = B.dma(wos[r0:r0 + 128, 1024:2048], woslo,
                                           waits=[t_lo], stream="ww1")
                wos_state["tok"] = t_lo
                yield

    wos_it = wos_units()
    VSTG = [180224, 182272]
    VEH = [184320, 185344, 186368, 187392]
    VEL = [188416, 189440, 190464, 191488]
    vstg_free = [None, None]
    veh_free = [None] * 4
    vel_free = [None] * 4
    vt4 = vt.rearrange("(kc hh p) c -> kc hh p c", hh=2, p=128)

    v_pending = []

    def v_emit(oc, ss, bk, ei, t_cp):
        hcv = oc - 32
        hh, pos = hcv // 8, (hcv % 8) * 128
        vstg = bf(VSTG[ei], 512)
        ei4 = (oc * 4 + ss) % 4
        tf = tern_final if oc == 32 else []
        vehw = bb(VEH[ei4], 512)
        velw = bb(VEL[ei4], 512)
        t_trs = []
        for j in range(4):
            t_trs.append(B.op("tensor", lambda e, bk=bk, j=j, vstg=vstg: e.matmul(
                pb[bk][:, j * 128:(j + 1) * 128], vstg[:, j * 128:(j + 1) * 128],
                idn, start=(j == 0), stop=(j == 3), is_transpose=True),
                waits=[t_cp] if j == 0 else (), inc="sP" if j == 3 else None))
        t_his = None
        for j in range(4):
            t_his = B.op("scalar", lambda e, bk=bk, j=j, vehw=vehw: e.activation(
                vehw[:, j * 128:(j + 1) * 128],
                pb[bk][:, j * 128:(j + 1) * 128], AF.Copy),
                waits=([t_trs[3], veh_free[ei4]] + tf) if j == 0 else (),
                inc="sA" if j == 3 else None)
        t_last_lo = None
        for j in range(4):
            t_last_lo = B.op("vector", lambda e, bk=bk, j=j, vehw=vehw, velw=velw: e.tensor_tensor(
                velw[:, j * 128:(j + 1) * 128],
                pb[bk][:, j * 128:(j + 1) * 128],
                vehw[:, j * 128:(j + 1) * 128], op=AL.subtract),
                waits=([t_trs[3], t_his, vel_free[ei4]] + tf) if j == 0 else (),
                inc="sV" if j == 3 else None)
        dst_hi = vt4[ss * 4:ss * 4 + 4, hh, :, pos:pos + 128].transpose([1, 0, 2])
        dst_lo = vt4[ss * 4:ss * 4 + 4, hh, :, 1024 + pos:1024 + pos + 128].transpose([1, 0, 2])
        veh_free[ei4] = B.dma(dst_hi, vehw, waits=[t_his], stream="vh%d" % ei4)
        vel_free[ei4] = B.dma(dst_lo, velw, waits=[t_last_lo], stream="vl%d" % ei4)
        vstg_free[ei] = t_trs[3]
        bank_free[bk] = t_last_lo

    qk_load(0)
    qk_load(1)
    qk_split(0)
    for oc in range(48):
        whi = bb(WQH[oc % 2], 2048)
        wlo = bb(WQL[oc % 2], 2048)
        t_hi, t_lo = split_tok[oc % 2]
        if oc + 1 < 48:
            qk_split(oc + 1)
        if oc + 2 < 48:
            qk_load(oc + 2)
        t_mm = None
        for ss in range(4):
            bk = ss + 4 * (oc % 2)
            n_mm = 0
            for hc in range(16):
                for (a, bmat) in ((whi, xh), (wlo, xh), (whi, xl)):
                    first = (n_mm == 0)
                    last = (n_mm == 47)
                    t_mm = B.op("tensor", lambda e, a=a, bmat=bmat, hc=hc, ss=ss, bk=bk, first=first, last=last: e.matmul(
                        pb[bk], a[:, hc * 128:(hc + 1) * 128],
                        bmat[:, hc * 2048 + ss * 512: hc * 2048 + ss * 512 + 512],
                        start=first, stop=last),
                        waits=[t_lo, t_hi, bank_free[bk]] if first else (),
                        inc="sP" if last else None)
                    n_mm += 1
            ei = (oc * 4 + ss) % 2
            if oc < 32:
                ehi = bb(EH[ei], 512)
                elo = bb(EL[ei], 512)
                t_ehi = B.op("scalar", lambda e, bk=bk, oc=oc, h=ehi: e.activation(
                    h, pb[bk], AF.Identity, bias=bqkv_sb[:, oc:oc + 1]),
                    waits=[t_mm, t_b1, ev_free[ei]], inc="sA")
                t_elo = B.op("vector", lambda e, bk=bk, oc=oc, h=ehi, l=elo: e.scalar_tensor_tensor(
                    l, pb[bk], bqkv_sb[:, oc:oc + 1], h, op0=AL.add, op1=AL.subtract),
                    waits=[t_mm, t_ehi, ev_free[ei]], inc="sV")
                bank_free[bk] = t_elo
                if oc < 16:
                    r0 = (oc * 4 + ss) * 128
                    B.dma(qf[r0:r0 + 128, 0:512], ehi, waits=[t_ehi],
                          stream="qw%d" % ei)
                    t_w2 = B.dma(qf[r0:r0 + 128, 512:1024], elo, waits=[t_elo],
                                 stream="qw%d" % ei)
                else:
                    hk = oc - 16
                    kh, pos = ss // 2, (ss % 2) * 512
                    r0 = (hk * 2 + kh) * 128
                    B.dma(kf[r0:r0 + 128, pos:pos + 512], ehi, waits=[t_ehi],
                          stream="qw%d" % ei)
                    t_w2 = B.dma(kf[r0:r0 + 128, 1024 + pos:1024 + pos + 512], elo,
                                 waits=[t_elo], stream="qw%d" % ei)
                ev_free[ei] = t_w2
            else:
                hcv = oc - 32
                ei = (oc * 4 + ss) % 2
                vstg = bf(VSTG[ei], 512)
                t_cp = B.op("scalar", lambda e, bk=bk, vstg=vstg: e.copy(
                    vstg, pb[bk]),
                    waits=[t_mm, vstg_free[ei]] +
                    (tern_final if oc == 32 and ss < 2 else []), inc="sA")
                pend = v_pending.pop(0) if v_pending else None
                v_pending.append((oc, ss, bk, ei, t_cp))
                if pend is not None:
                    v_emit(*pend)
            if ss == 3:
                prev_pair[oc % 2] = t_mm
                prev_wqf[oc % 2] = t_lo
            if oc < 32:
                pump(3)
            else:
                if ss == 1 or ss == 3:
                    next(wos_it, None)
    for pend in v_pending:
        v_emit(*pend)
    v_pending = []
    for _ in tern_it:
        pass
    for _ in wos_it:
        pass

    s3_end = [B.stream_last.get(s) for s in
              ("qw0", "qw1", "ww0", "ww1", "vh0", "vh1", "vh2", "vh3",
               "vl0", "vl1", "vl2", "vl3", "tw0", "tw1")] + \
        [B.last_tok["tensor"], B.last_tok["scalar"], B.last_tok["vector"]]

    QSL, SCO, PTH, PTL = 0, 32768, 65536, 81920
    KB = [98304, 102400]
    ESB, PN = 106496, 114688
    VB = [122880, 126976]
    CTH, CTL = 131072, 147456
    WOB = [163840, 167936]
    XRS = 98304
    YB, Y1B = 32768, 0
    DMP = 106496
    Y1FH, Y1FL = 65536, 81920
    QFCS = 131072
    RLS = [163840, 32768]
    T2S = [165888, 34816]
    QPB = [135168, 137216]
    OUTB = [65536, 73728]

    def hsq_off(hc):
        if hc < 16:
            return 98304 + hc * 2048
        if hc < 24:
            return 147456 + (hc - 16) * 2048
        return 167936 + (hc - 24) * 2048

    qsl = bb(QSL, 16384)
    sco = bf(SCO, 8192)
    pth = bb(PTH, 8192)
    ptl = bb(PTL, 8192)
    esb = bf(ESB, 2048)
    pn = bf(PN, 2048)
    cth = bb(CTH, 8192)
    ctl = bb(CTL, 8192)
    xrs = bf(XRS, 8192)
    yb = bf(YB, 8192)
    y1b = bf(Y1B, 8192)
    dmp = bf(DMP, 2048)
    y1fh = bb(Y1FH, 8192)
    y1fl = bb(Y1FL, 8192)
    qfcs = bb(QFCS, 8192)
    rls = [bf(RLS[0], 512), bf(RLS[1], 512)]
    t2s = [bf(T2S[0], 512), bf(T2S[1], 512)]

    grp_end = {k: None for k in ("y2v", "hap", "hlv", "out0", "out1", "pe",
                                 "kbfree")}
    grp_end["y2map"] = {}
    for g in range(4):
        s30 = s3_end if g == 0 else []

        def ge(*keys):
            return [grp_end[k] for k in keys]

        kb_slot_last = [None, None]
        t_kld0 = {}
        for hc in range(2):
            ki = hc % 2
            r0 = (hc * 2 + 0) * 128
            t_kld0[hc] = B.dma(bb(KB[ki], 2048), kf[r0:r0 + 128, :],
                               waits=ge("kbfree") + (s30 if hc == 0 else []),
                               stream="kbg%d" % ki, eng="gpsimd")
        t_qsl = [None] * 16
        hq_order = [h for h in range(16) if h % 4 < 2] + \
            [h for h in range(16) if h % 4 >= 2]
        for idx, hq in enumerate(hq_order):
            r0 = (hq * 4 + g) * 128
            ytok = grp_end["y2map"].get((hq // 4, (hq % 4) // 2, hq % 2))
            t_qsl[hq] = B.dma(qsl[:, hq * 1024:(hq + 1) * 1024], qf[r0:r0 + 128, :],
                              waits=[ytok],
                              stream="qg%d" % hq, eng="gpsimd")
        t_sco_ev = []
        for kh in range(2):
            t_kld = dict(t_kld0) if kh == 0 else {}

            def kb_load(hc, kh=kh):
                ki = (kh * 16 + hc) % 2
                r0 = (hc * 2 + kh) * 128
                t_kld[hc] = B.dma(bb(KB[ki], 2048), kf[r0:r0 + 128, :],
                                  waits=[kb_slot_last[ki]],
                                  stream="kb%d" % ki)

            if kh == 1:
                kb_load(0)
                kb_load(1)
            for hc in range(16):
                ki = (kh * 16 + hc) % 2
                kb = bb(KB[ki], 2048)
                t_hcmm = None
                for qt in range(4):
                    for ks in range(2):
                        bk = qt * 2 + ks
                        qoff = hc * 1024 + qt * 128
                        for ip, (a_off, b_off) in enumerate(((qoff, ks * 512),
                                                            (qoff + 512, ks * 512),
                                                            (qoff, 1024 + ks * 512))):
                            first = (hc == 0 and ip == 0)
                            end_of_acc = (hc == 15 and ip == 2)
                            end_of_kb = (qt == 3 and ks == 1 and ip == 2)
                            w = []
                            if first:
                                w = [bank_free[bk]]
                            if ip == 0 and qt == 0 and ks == 0:
                                w = list(w) + [t_kld[hc], t_qsl[hc]]
                            t_hcmm = B.op("tensor", lambda e, a_off=a_off, b_off=b_off, bk=bk, kb=kb, first=first, last=end_of_acc: e.matmul(
                                pb[bk], qsl[:, a_off:a_off + 128], kb[:, b_off:b_off + 512],
                                start=first, stop=last),
                                waits=w,
                                inc="sP" if (end_of_acc or end_of_kb) else None)
                            if end_of_acc:
                                acc_tok = t_hcmm
                                bank_free[bk] = None
                                t_sco_ev.append((bk, qt, ks, acc_tok))
                kb_slot_last[ki] = t_hcmm
                if hc + 2 < 16:
                    kb_load(hc + 2)
            ev_toks = []
            for (bk, qt, ks, acc_tok) in t_sco_ev[-8:]:
                dst = qt * 2048 + kh * 1024 + ks * 512
                eng = "scalar" if bk % 2 == 0 else "vector"
                exw = ge("hap", "hlv") if (kh == 0 and bk == 0) else []
                if eng == "scalar":
                    tok = B.op(eng, lambda e, bk=bk, dst=dst: e.copy(
                        sco[:, dst:dst + 512], pb[bk]),
                        waits=[acc_tok] + exw, inc="sA")
                else:
                    tok = B.op(eng, lambda e, bk=bk, dst=dst: e.tensor_copy(
                        sco[:, dst:dst + 512], pb[bk]),
                        waits=[acc_tok] + exw, inc="sV")
                bank_free[bk] = tok
                ev_toks.append(tok)
        sco_qt_toks = {qt: [] for qt in range(4)}
        for (bk, qt_, ks_, _a) in t_sco_ev:
            sco_qt_toks[qt_].append(bank_free[bk])
        t_sco_all = [t for t in (ev_toks + [bank_free[i] for i in range(8)])
                     if t is not None]
        t_qsl_lastread = kb_slot_last[1] or kb_slot_last[0]

        t_pt_done = []
        pn2 = [esb, pn]
        pn_read = [None, None]
        pn_norm = [None, None]
        t_tr = None
        for qt in range(4):
            pnb = pn2[qt % 2]
            t_nm = B.op("vector", lambda e, qt=qt: e.tensor_reduce(
                vcol(NMAX), sco[:, qt * 2048:(qt + 1) * 2048], axis=AX.X, op=AL.max,
                negate=True), waits=sco_qt_toks[qt] + [pn_norm[qt % 2]],
                inc="sV")
            t_eb = B.op("vector", lambda e: e.tensor_tensor(
                vcol(EBIAS), vcol(NMAX), vcol(SSCB), op=AL.mult),
                waits=[t_sscb], inc="sV")
            t_exp = B.op("scalar", lambda e, qt=qt, pnb=pnb: e.activation(
                pnb, sco[:, qt * 2048:(qt + 1) * 2048], AF.Exp,
                bias=vcol(EBIAS), scale=vcol(SSCB), accum_out=vcol(ESUM)),
                waits=[t_eb, pn_read[qt % 2]] + (ge("pe") if qt == 0 else []),
                inc="sA")
            t_rq = B.op("vector", lambda e: e.reciprocal(vcol(RQ), vcol(ESUM)),
                        waits=[t_exp], inc="sV")
            t_pn_c = []
            for c in range(4):
                t_pn_c.append(B.op("vector", lambda e, c=c, pnb=pnb: e.tensor_single_scalar(
                    pnb[:, c * 512:(c + 1) * 512], pnb[:, c * 512:(c + 1) * 512],
                    vcol(RQ), op=AL.mult),
                    waits=[t_rq] if c == 0 else (), inc="sV"))
            pn_norm[qt % 2] = t_pn_c[-1]
            for kc4 in range(4):
                bk = kc4 % 2
                t_tr = None
                for j in range(4):
                    kc = kc4 * 4 + j
                    t_tr = B.op("tensor", lambda e, kc=kc, j=j, bk=bk, pnb=pnb: e.matmul(
                        pb[bk][:, j * 128:(j + 1) * 128],
                        pnb[:, kc * 128:(kc + 1) * 128], idn,
                        start=(j == 0), stop=(j == 3), is_transpose=True),
                        waits=[t_pn_c[kc4], bank_free[bk]] if j == 0 else (),
                        inc="sP" if j == 3 else None)
                dsth = pth.rearrange("p (kc c) -> p kc c", c=512)[
                    :, kc4 * 4:kc4 * 4 + 4, qt * 128:qt * 128 + 128]
                dstl = ptl.rearrange("p (kc c) -> p kc c", c=512)[
                    :, kc4 * 4:kc4 * 4 + 4, qt * 128:qt * 128 + 128]
                t_hi = B.op("scalar", lambda e, bk=bk, dsth=dsth: e.activation(
                    dsth, pb[bk], AF.Copy),
                    waits=[t_tr] + (ge("out0", "out1") if qt == 0 and kc4 == 0 else []),
                    inc="sA")
                t_lo = B.op("vector", lambda e, bk=bk, dsth=dsth, dstl=dstl: e.tensor_tensor(
                    dstl, pb[bk], dsth, op=AL.subtract),
                    waits=[t_tr, t_hi], inc="sV")
                bank_free[bk] = t_lo
                t_pt_done.append(t_lo)
            pn_read[qt % 2] = t_tr
        t_sco_lastread = t_exp
        prev_pn_read = t_tr

        t_ct_done = []
        vb_slot_last = [None, None]
        for hh in range(2):
            t_vld = {}

            def vb_load(kc, hh=hh):
                vi = (hh * 16 + kc) % 2
                r0 = (kc * 2 + hh) * 128
                t_vld[kc] = B.dma(bb(VB[vi], 2048), vt[r0:r0 + 128, :],
                                  waits=[vb_slot_last[vi]] +
                                  (ge("pe") if hh == 0 and kc < 2 else []),
                                  stream="vb%d" % vi)

            vb_load(0)
            vb_load(1)
            t_kcmm = {}
            for kc in range(16):
                vi = (hh * 16 + kc) % 2
                vbuf = bb(VB[vi], 2048)
                t_last = None
                for h8 in range(8):
                    bk = h8
                    for ip in range(3):
                        a_off = h8 * 128 if ip != 1 else 1024 + h8 * 128
                        pmat = pth if ip != 2 else ptl
                        first = (kc == 0 and ip == 0)
                        end_of_acc = (kc == 15 and ip == 2)
                        end_of_vb = (h8 == 7 and ip == 2)
                        w = []
                        if first:
                            w = [t_pt_done[-1], bank_free[bk]]
                        if ip == 0 and h8 == 0:
                            w = list(w) + [t_vld[kc]]
                        t_last = B.op("tensor", lambda e, vbuf=vbuf, a_off=a_off, kc=kc, pmat=pmat, bk=bk, first=first, last=end_of_acc: e.matmul(
                            pb[bk], vbuf[:, a_off:a_off + 128],
                            pmat[:, kc * 512:kc * 512 + 512],
                            start=first, stop=last),
                            waits=w, inc="sP" if (end_of_acc or end_of_vb) else None)
                        if end_of_acc:
                            t_kcmm[bk] = t_last
                vb_slot_last[vi] = t_last
                if kc + 2 < 16:
                    vb_load(kc + 2)
            for h8 in range(8):
                hca = hh * 8 + h8
                dst = hca * 512
                t_hi = B.op("scalar", lambda e, bk=h8, dst=dst, hca=hca: e.activation(
                    cth[:, dst:dst + 512], pb[bk], AF.Identity,
                    bias=bqkv_sb[:, 32 + hca:33 + hca]),
                    waits=[t_kcmm[h8]], inc="sA")
                t_lo = B.op("vector", lambda e, bk=h8, dst=dst, hca=hca: e.scalar_tensor_tensor(
                    ctl[:, dst:dst + 512], pb[bk], bqkv_sb[:, 32 + hca:33 + hca],
                    cth[:, dst:dst + 512], op0=AL.add, op1=AL.subtract),
                    waits=[t_kcmm[h8], t_hi], inc="sV")
                bank_free[h8] = t_lo
                t_ct_done.append(t_lo)
        t_pt_lastread = vb_slot_last[1] or vb_slot_last[0]

        t_wld = {0: {}, 1: {}}
        wob_slot_last = [None, None]

        def wob_load(hc, oh):
            wi = (oh * 16 + hc) % 2
            r0 = (hc * 2 + oh) * 128
            t_wld[oh][hc] = B.dma(bb(WOB[wi], 2048), wos[r0:r0 + 128, :],
                                  waits=[wob_slot_last[wi]] +
                                  (ge("pe", "y2v") if oh == 0 and hc < 2 else []),
                                  stream="wo%d" % wi)

        wob_load(0, 0)
        wob_load(1, 0)
        t_xld = []
        for sci in range(4):
            t_xld.append(B.dma(xrs[:, sci * 2048:(sci + 1) * 2048],
                               x_in[g * 512 + sci * 128: g * 512 + sci * 128 + 128, :],
                               waits=[t_pt_lastread if sci == 0 else None,
                                      prev_pn_read if sci == 0 else None,
                                      t_ct_done[-1] if sci == 0 else None],
                               stream="xs"))
        t_y_done = []
        y_map = {}
        t_d_lastmm = None
        for oh in range(2):
            if oh == 1:
                wob_load(0, 1)
                wob_load(1, 1)
            for hc in range(16):
                wi = (oh * 16 + hc) % 2
                wbuf = bb(WOB[wi], 2048)
                t_last = None
                for sci in range(4):
                    for os2 in range(2):
                        bk = sci * 2 + os2
                        coff = hc * 512 + sci * 128
                        for ip in range(3):
                            amat = cth if ip != 1 else ctl
                            b_off = os2 * 512 if ip != 2 else 1024 + os2 * 512
                            first = (hc == 0 and ip == 0)
                            end_of_wb = (sci == 3 and os2 == 1 and ip == 2)
                            w = []
                            if first:
                                w = [t_ct_done[-1], bank_free[bk]]
                            if ip == 0 and sci == 0 and os2 == 0:
                                w = list(w) + [t_wld[oh][hc]]
                            t_last = B.op("tensor", lambda e, amat=amat, coff=coff, wbuf=wbuf, b_off=b_off, bk=bk, first=first: e.matmul(
                                pb[bk], amat[:, coff:coff + 128], wbuf[:, b_off:b_off + 512],
                                start=first, stop=False),
                                waits=w, inc="sP" if end_of_wb else None)
                wob_slot_last[wi] = t_last
                t_d_lastmm = t_last
                if hc + 2 < 16:
                    wob_load(hc + 2, oh)
            for bk in range(8):
                sci, os2 = bk // 2, bk % 2
                boff = oh * 1024 + os2 * 512
                B.op("tensor", lambda e, bk=bk, boff=boff: e.matmul(
                    pb[bk], ones_bf, bout_hi[:, boff:boff + 512],
                    start=False, stop=False), waits=[t_bh, t_g0] if g == 0 and oh == 0 and bk == 0 else ())
                t_bias = B.op("tensor", lambda e, bk=bk, boff=boff: e.matmul(
                    pb[bk], ones_bf, bout_lo[:, boff:boff + 512],
                    start=False, stop=True), waits=[t_bl] if g == 0 and oh == 0 and bk == 0 else (), inc="sP")
                dst = sci * 2048 + oh * 1024 + os2 * 512
                t_y = B.op("vector", lambda e, bk=bk, dst=dst: e.tensor_tensor(
                    yb[:, dst:dst + 512], pb[bk], xrs[:, dst:dst + 512], op=AL.add),
                    waits=[t_bias, t_xld[-1],
                           t_sco_lastread if oh == 0 and bk == 0 else None], inc="sV")
                bank_free[bk] = t_y
                y_map[(sci, oh, os2)] = t_y
                t_y_done.append(t_y)

        def layer_norm(src_view, dst_view, waits_first, sci, dump_extra=None, apply_chunks=1):
            base = 64 + sci * 9
            NS, NMc, SSQc, T1c, T2c = base, base + 1, base + 2, base + 3, base + 4
            Vc, SDc, RSc, NMRc = base + 5, base + 6, base + 7, base + 8
            t_ns = B.op("vector", lambda e: e.tensor_reduce(
                vcol(NS), src_view, axis=AX.X, op=AL.add, negate=True),
                waits=waits_first, inc="sV")
            B.op("vector", lambda e: e.tensor_single_scalar(
                vcol(NMc), vcol(NS), INV2048, op=AL.mult))
            t_sq_ = B.op("scalar", lambda e: e.activation(
                dmp, src_view, AF.Square, accum_out=vcol(SSQc)),
                waits=list(waits_first) + [dump_extra], inc="sA")
            B.op("vector", lambda e: e.tensor_single_scalar(
                vcol(T1c), vcol(SSQc), INV2048, op=AL.mult), waits=[t_sq_])
            B.op("vector", lambda e: e.tensor_tensor(vcol(T2c), vcol(NMc), vcol(NMc),
                                                     op=AL.mult))
            t_var = B.op("vector", lambda e: e.tensor_tensor(
                vcol(Vc), vcol(T1c), vcol(T2c), op=AL.subtract), inc="sV")
            t_sd = B.op("scalar", lambda e: e.activation(
                vcol(SDc), vcol(Vc), AF.Sqrt, bias=vcol(EPSB)),
                waits=[t_var, t_epsb], inc="sA")
            t_rs = B.op("vector", lambda e: e.reciprocal(vcol(RSc), vcol(SDc)),
                        waits=[t_sd], inc="sV")
            t_nmr = B.op("vector", lambda e: e.tensor_tensor(
                vcol(NMRc), vcol(NMc), vcol(RSc), op=AL.mult), waits=[t_rs], inc="sV")
            t_apc = []
            for c in range(apply_chunks):
                w_ = 2048 // apply_chunks
                t_apc.append(B.op("scalar", lambda e, c=c, w_=w_: e.activation(
                    dst_view[:, c * w_:(c + 1) * w_], src_view[:, c * w_:(c + 1) * w_],
                    AF.Identity, bias=vcol(NMRc), scale=vcol(RSc)),
                    waits=[t_nmr, t_rs] if c == 0 else (), inc="sA"))
            return t_apc, t_nmr

        t_y1_tr = []
        t_ln_prev = None
        ln1_vtok = [None] * 4
        for sci in range(4):
            y_toks = [y_map[(sci, oh_, os_)] for oh_ in range(2) for os_ in range(2)]
            t_apc, _ = layer_norm(
                yb[:, sci * 2048:(sci + 1) * 2048],
                y1b[:, sci * 2048:(sci + 1) * 2048],
                y_toks + [t_qsl_lastread if sci == 0 else None],
                sci, dump_extra=t_y_done[-1] if sci == 0 else None, apply_chunks=4)
            ln1_vtok[sci] = _
            t_ln_prev = t_apc[-1]
            for dc4 in range(4):
                bk = (sci * 4 + dc4) % 2
                t_tr = None
                for j in range(4):
                    src_off = sci * 2048 + (dc4 * 4 + j) * 128
                    t_tr = B.op("tensor", lambda e, src_off=src_off, j=j, bk=bk: e.matmul(
                        pb[bk][:, j * 128:(j + 1) * 128],
                        y1b[:, src_off:src_off + 128], idn,
                        start=(j == 0), stop=(j == 3), is_transpose=True),
                        waits=[t_apc[dc4], bank_free[bk]] if j == 0 else (),
                        inc="sP" if j == 3 else None)
                dsth = y1fh.rearrange("p (dc c) -> p dc c", c=512)[
                    :, dc4 * 4:dc4 * 4 + 4, sci * 128:sci * 128 + 128]
                dstl = y1fl.rearrange("p (dc c) -> p dc c", c=512)[
                    :, dc4 * 4:dc4 * 4 + 4, sci * 128:sci * 128 + 128]
                t_hi = B.op("scalar", lambda e, bk=bk, dsth=dsth: e.activation(
                    dsth, pb[bk], AF.Copy), waits=[t_tr], inc="sA")
                t_lo = B.op("vector", lambda e, bk=bk, dsth=dsth, dstl=dstl: e.tensor_tensor(
                    dstl, pb[bk], dsth, op=AL.subtract),
                    waits=[t_tr, t_hi], inc="sV")
                bank_free[bk] = t_lo
                t_y1_tr.append(t_lo)

        t_hsq_done = []
        t_qf_free = None
        t_rl_free = [None, None]
        t_t2_free = [None, None]
        for hcb in range(10):
            t_qld = []
            for dc in range(16):
                t_qld.append(B.dma(qfcs[:, dc * 512:(dc + 1) * 512],
                             qfc[dc * 128:(dc + 1) * 128, hcb * 512:(hcb + 1) * 512],
                             waits=[t_qf_free if dc == 0 else None,
                                    t_d_lastmm if hcb == 0 and dc == 0 else None] +
                             (tern_done[0] if g == 0 and hcb == 0 and dc == 0 else []),
                             stream="ts%d" % dc))
            t_mmf = {}
            for h4 in range(4):
                bk = h4 + 4 * (hcb % 2)
                n_mm = 0
                for dc in range(16):
                    for amat in (y1fh, y1fl):
                        first = (n_mm == 0)
                        last = (n_mm == 31)
                        w = []
                        if first:
                            w = [t_y1_tr[-1], bank_free[bk]]
                        if h4 == 0 and amat is y1fh:
                            w = list(w) + [t_qld[dc]]
                        t_mmf[bk] = B.op("tensor", lambda e, dc=dc, h4=h4, amat=amat, bk=bk, first=first, last=last: e.matmul(
                            pb[bk], qfcs[:, dc * 512 + h4 * 128: dc * 512 + h4 * 128 + 128],
                            amat[:, dc * 512:(dc + 1) * 512],
                            start=first, stop=last),
                            waits=w, inc="sP" if last else None)
                        n_mm += 1
                hca = hcb * 4 + h4
                hoff = hsq_off(hca)
                pi = hca % 2
                rl = rls[pi]
                t2 = t2s[pi]
                t_rl = B.op("scalar", lambda e, bk=bk, rl=rl: e.activation(
                    rl, pb[bk], AF.Relu, scale=vcol(SFCB)),
                    waits=[t_mmf[bk], tern_scale[0], t_rl_free[pi]] +
                    ([ln1_vtok[0]] if hca < 2 else []), inc="sA")
                t_t2 = B.op("vector", lambda e, rl=rl, t2=t2: e.tensor_tensor(
                    t2, rl, rl, op=AL.mult),
                    waits=[t_rl, t_t2_free[pi]], inc="sV")
                t_hi = B.op("scalar", lambda e, hoff=hoff, t2=t2: e.activation(
                    bb(hoff, 512), t2, AF.Copy), waits=[t_t2], inc="sA")
                t_lo = B.op("vector", lambda e, hoff=hoff, t2=t2: e.tensor_tensor(
                    bb(hoff + 1024, 512), t2, bb(hoff, 512), op=AL.subtract),
                    waits=[t_hi], inc="sV")
                bank_free[bk] = t_rl
                t_rl_free[pi] = t_t2
                t_t2_free[pi] = t_lo
                t_hsq_done.append(t_lo)
            t_qf_free = t_mmf[3 + 4 * (hcb % 2)]

        t_y2_done = []
        g_y2map = {}
        g_kbfree_new = [None]
        qp_slot_last = [t_qf_free, t_qf_free]
        t_mmp2 = {}
        for dh in range(2):
            t_qpld = {}

            def qp_load(hc, dh=dh):
                qi = (dh * FC + hc) % 2
                t_qpld[hc] = B.dma(bb(QPB[qi], 1024),
                                   qpr[hc * 128:(hc + 1) * 128, dh * 1024:(dh + 1) * 1024],
                                   waits=[qp_slot_last[qi]] +
                                   (tern_done[1] if g == 0 and dh == 0 and hc < 2 else []),
                                   stream="qp%d" % qi)

            qp_load(0)
            qp_load(1)
            for hc in range(FC):
                qi = (dh * FC + hc) % 2
                qpb = bb(QPB[qi], 1024)
                hoff = hsq_off(hc)
                t_last = None
                for sci in range(4):
                    for ds in range(2):
                        bk = sci * 2 + ds
                        for ip in range(2):
                            a_ap = bb(hoff + (0 if ip == 0 else 1024), 512)
                            first = (hc == 0 and ip == 0)
                            end_of_acc = (hc == FC - 1 and ip == 1)
                            end_of_qp = (sci == 3 and ds == 1 and ip == 1)
                            w = []
                            if first:
                                w = [t_hsq_done[-1], bank_free[bk]]
                            if ip == 0 and sci == 0 and ds == 0:
                                w = list(w) + [t_qpld[hc]]
                            t_last = B.op("tensor", lambda e, a_ap=a_ap, sci=sci, qpb=qpb, ds=ds, bk=bk, first=first, last=end_of_acc: e.matmul(
                                pb[bk], a_ap[:, sci * 128:sci * 128 + 128],
                                qpb[:, ds * 512:ds * 512 + 512],
                                start=first, stop=last),
                                waits=w, inc="sP" if (end_of_acc or end_of_qp) else None)
                            if end_of_acc:
                                t_mmp2[bk] = t_last
                qp_slot_last[qi] = t_last
                if dh == 1 and hc == 3:
                    g_kbfree_new[0] = t_last
                if hc + 2 < FC:
                    qp_load(hc + 2)
            for bk in range(8):
                sci, ds = bk // 2, bk % 2
                dst = sci * 2048 + dh * 1024 + ds * 512
                t_y2 = B.op("vector", lambda e, bk=bk, dst=dst: e.scalar_tensor_tensor(
                    yb[:, dst:dst + 512], pb[bk], vcol(SPRB), y1b[:, dst:dst + 512],
                    op0=AL.mult, op1=AL.add),
                    waits=[t_mmp2[bk], tern_scale[1]], inc="sV")
                bank_free[bk] = t_y2
                g_y2map[(sci, dh, ds)] = t_y2
                t_y2_done.append(t_y2)

        t_ln_prev = None
        t_ln_v = None
        outb_free = [None, None]
        t_out_wr = None
        for sci in range(4):
            ob = bf(OUTB[sci % 2], 2048)
            y2_toks = [g_y2map[(sci, dh_, ds_)] for dh_ in range(2) for ds_ in range(2)]
            t_apc, t_ln_v = layer_norm(
                yb[:, sci * 2048:(sci + 1) * 2048], ob,
                y2_toks + [outb_free[sci % 2]],
                sci)
            t_ln_prev = t_apc[-1]
            t_out_wr = B.dma(out_hbm[g * 512 + sci * 128: g * 512 + sci * 128 + 128, :],
                             ob, waits=[t_apc[-1]], stream="ou%d" % (sci % 2))
            outb_free[sci % 2] = t_out_wr
        grp_end = {"y2v": t_y2_done[-1], "hap": t_ln_prev, "hlv": t_ln_v,
                   "out0": outb_free[0], "out1": outb_free[1],
                   "pe": t_mmp2[7], "y2map": g_y2map, "kbfree": g_kbfree_new[0]}

    assert nc.sbuf_base <= my_base, (nc.sbuf_base, my_base)
    B.op("sync", lambda e: e.nop(),
         waits=[("D_ou0", B.plan["D_ou0"], "x"), ("D_ou1", B.plan["D_ou1"], "x")])
    B.finalize()
    return nc


_NC_CACHE = []


def kernel(L0_attn_hidden_states, L0_attn_w_qkv, L0_attn_b_qkv, L0_attn_w_out,
           L0_attn_b_out, L0_ffn_w_fc, L0_ffn_w_proj):
    x = np.asarray(L0_attn_hidden_states, dtype=np.float32)
    wqkv = np.asarray(L0_attn_w_qkv, dtype=np.float32)
    bqkv = np.asarray(L0_attn_b_qkv, dtype=np.float32)
    wout = np.asarray(L0_attn_w_out, dtype=np.float32)
    bout = np.asarray(L0_attn_b_out, dtype=np.float32)
    wfc = np.asarray(L0_ffn_w_fc, dtype=np.float32)
    wproj = np.asarray(L0_ffn_w_proj, dtype=np.float32)

    w1 = wqkv.reshape(48, 128, 16, 128)
    wqk_t = np.ascontiguousarray(w1.transpose(0, 3, 2, 1)).reshape(48 * 128, 2048)
    bqkv_t = np.ascontiguousarray(bqkv.reshape(48, 128).T)
    bout_r = np.ascontiguousarray(bout.reshape(1, 2048))
    wot_t = np.ascontiguousarray(wout.T).reshape(16 * 128, 2048)
    wfct_t = np.ascontiguousarray(wfc.T).reshape(16 * 128, 5120)
    wprt_t = np.ascontiguousarray(wproj.T).reshape(40 * 128, 2048)

    if not _NC_CACHE:
        _NC_CACHE.append(build_core())
    nc = _NC_CACHE[0]
    shared = dict(wqk=wqk_t, bqkv=bqkv_t, bout=bout_r,
                  wot=wot_t, wfct=wfct_t, wprt=wprt_t)
    in_maps = [dict(shared, x=np.ascontiguousarray(x[i])) for i in range(8)]
    res = run_bass_kernel_spmd(nc, in_maps, core_ids=list(range(8)))
    return np.stack([np.asarray(res.results[i]["out"]) for i in range(8)], axis=0)
